# version 01
# baseline (speedup 1.0000x reference)
import contextlib
import numpy as np
import ml_dtypes
import concourse.bass as bass
import concourse.mybir as mybir
from concourse.bass_utils import run_bass_kernel_spmd

F32 = mybir.dt.float32
BF16 = mybir.dt.bfloat16
I8 = mybir.dt.int8
AF = mybir.ActivationFunctionType
ALU = mybir.AluOpType
AX = mybir.AxisListType

NCORES = 8
D = 2048
T = 8192
TC = T // NCORES
HD = 64
NQH = 4
SC = float(np.sqrt(np.float64(0.125)))
EPS = 1e-5
INTER = 8192
RG = [list(range(NCORES))]

ENGS = ("sync", "gpsimd", "tensor", "scalar", "vector")
SEMS = ("IN", "WL", "SO", "PE", "ACT", "DVE", "CC", "BAR")


class Sch:
    def __init__(self):
        self.prog = {e: [] for e in ENGS}
        self.n = {s: 0 for s in SEMS}
        self.lw = {e: {s: 0 for s in SEMS} for e in ENGS}

    def wait(self, eng, sem, val):
        if not val or val <= self.lw[eng][sem]:
            return
        self.lw[eng][sem] = val
        self.prog[eng].append(("w", sem, val))

    def op(self, eng, fn, inc=None, amt=1):
        if inc:
            self.n[inc] += amt
            self.prog[eng].append(("o", fn, inc, amt))
            return self.n[inc]
        self.prog[eng].append(("o", fn, None, 0))
        return None

    def dma(self, eng, fn, inc="IN"):
        return self.op(eng, fn, inc, 16)

    def multi(self, eng, fn, inc, amt):
        self.n[inc] += amt
        self.prog[eng].append(("m", fn, inc, amt))
        return self.n[inc]

    def barrier(self):
        tgt = self.n["BAR"] + len(ENGS)
        for e in ENGS:
            self.op(e, lambda E: E.nop(), "BAR", 1)
        for e in ENGS:
            self.wait(e, "BAR", tgt)

    def emit(self, block, sems):
        def mkbody(name):
            def run(E):
                for item in self.prog[name]:
                    if item[0] == "w":
                        E.wait_ge(sems[item[1]], item[2])
                    elif item[0] == "m":
                        item[1](E, sems[item[2]])
                    else:
                        _, fn, inc, amt = item
                        instr = fn(E)
                        if inc:
                            instr.then_inc(sems[inc], amt)
            return run

        block.sync(mkbody("sync"))
        block.gpsimd(mkbody("gpsimd"))
        block.tensor(mkbody("tensor"))
        block.scalar(mkbody("scalar"))
        block.vector(mkbody("vector"))


def build_nc(dbg_stop=None):
    nc = bass.Bass("TRN2", target_bir_lowering=False, debug=False)

    xt = nc.dram_tensor("xt", [D, TC], F32, kind="ExternalInput").ap()
    wqkv = nc.dram_tensor("wqkv", [D, 384], F32, kind="ExternalInput").ap()
    wot = nc.dram_tensor("wot", [D, D], F32, kind="ExternalInput").ap()
    wg_t = nc.dram_tensor("wg_t", [INTER, D], I8, kind="ExternalInput").ap()
    wu_t = nc.dram_tensor("wu_t", [INTER, D], I8, kind="ExternalInput").ap()
    wd_t = nc.dram_tensor("wd_t", [512, 64 * 512], I8, kind="ExternalInput").ap()
    sgd = nc.dram_tensor("sgd", [128, 64], F32, kind="ExternalInput").ap()
    sud = nc.dram_tensor("sud", [128, 64], F32, kind="ExternalInput").ap()
    sdd = nc.dram_tensor("sdd", [128, 16], F32, kind="ExternalInput").ap()
    ctokd = nc.dram_tensor("ctokd", [128, 2048], F32, kind="ExternalInput").ap()
    stokd = nc.dram_tensor("stokd", [128, 2048], F32, kind="ExternalInput").ap()
    masksd = nc.dram_tensor("masksd", [128, 8 * 1024], BF16, kind="ExternalInput").ap()
    identd = nc.dram_tensor("identd", [128, 128], BF16, kind="ExternalInput").ap()
    cmiscd = nc.dram_tensor("cmiscd", [65, 256], F32, kind="ExternalInput").ap()
    col32d = nc.dram_tensor("col32d", [128, 1], F32, kind="ExternalInput").ap()
    colbd = nc.dram_tensor("colbd", [128, 1], BF16, kind="ExternalInput").ap()

    y = nc.dram_tensor("y", [D, TC], F32, kind="ExternalOutput").ap()

    cc1_in = nc.dram_tensor("cc1_in", [D, TC], BF16).ap()
    cc1_out = nc.dram_tensor("cc1_out", [NCORES * D, TC], BF16).ap()
    cc2_in = nc.dram_tensor("cc2_in", [NQH * HD, T], BF16).ap()
    cc2_out = nc.dram_tensor("cc2_out", [NCORES * NQH * HD, T], BF16).ap()
    x2d = nc.dram_tensor("x2d", [D, TC], F32).ap()

    dbg_outs = {}

    def dbg(name, shape, dtype=F32):
        ap = nc.dram_tensor(name, shape, dtype, kind="ExternalOutput").ap()
        dbg_outs[name] = ap
        return ap

    S = Sch()

    with contextlib.ExitStack() as root:
        _uid = [0]

        def sb(stack, name, shape, dtype, side=None):
            _uid[0] += 1
            return stack.enter_context(
                nc.sbuf_tensor(f"{name}_{_uid[0]}", shape, dtype, side=side))[:, :]

        def ps(stack, name, shape, dtype=F32):
            _uid[0] += 1
            return stack.enter_context(
                nc.psum_tensor(f"{name}_{_uid[0]}", shape, dtype))[:, :]

        sems = {s: root.enter_context(nc.semaphore(s)) for s in SEMS}

        s_ident = sb(root, "s_ident", [128, 128], BF16)
        s_cmisc = sb(root, "s_cmisc", [65, 256], F32)
        s_col32 = sb(root, "s_col32", [128, 1], F32)
        s_colb = sb(root, "s_colb", [128, 1], BF16)
        s_sg = sb(root, "s_sg", [128, 64], F32)
        s_su = sb(root, "s_su", [128, 64], F32)
        s_sd = sb(root, "s_sd", [128, 16], F32)
        r_a = sb(root, "r_a", [1, 1024], F32)
        r_b = sb(root, "r_b", [1, 1024], F32)
        r_c = sb(root, "r_c", [1, 1024], F32)
        s_wqkv = sb(root, "s_wqkv", [128, 16 * 384], BF16)
        s_km2 = sb(root, "s_km2", [65, 1], F32)

        for dst, src in ((s_ident, identd), (s_cmisc, cmiscd), (s_col32, col32d),
                         (s_colb, colbd), (s_sg, sgd), (s_su, sud), (s_sd, sdd)):
            S.dma("sync", lambda E, dst=dst, src=src: E.dma_start(out=dst, in_=src))
        S.op("vector", lambda E: E.memset(s_km2[64:65, :], 0.0), "DVE")

        for c in range(16):
            S.dma("gpsimd", lambda E, c=c: E.dma_start(
                out=s_wqkv[:, c * 384:(c + 1) * 384],
                in_=wqkv[c * 128:(c + 1) * 128, :]), "WL")
        wl_wqkv = S.n["WL"]

        def emit_ln_rows(p_sm, p_ssq, W, roff, p_rb, p_bb, s_rstdb, s_bb, pe_stats_done):
            sl = slice(roff, roff + W)
            S.wait("scalar", "PE", pe_stats_done)
            S.op("scalar", lambda E: E.activation(
                out=r_a[:, sl], in_=p_sm, func=AF.Copy, scale=1.0 / D), "ACT")
            a2 = S.op("scalar", lambda E: E.activation(
                out=r_b[:, sl], in_=p_sm, func=AF.Square, scale=1.0 / D), "ACT")
            S.wait("vector", "ACT", a2)
            S.wait("vector", "PE", pe_stats_done)
            d1 = S.op("vector", lambda E: E.tensor_scalar(
                out=r_c[:, sl], in0=p_ssq, scalar1=1.0 / D, scalar2=EPS,
                op0=ALU.mult, op1=ALU.add), "DVE")
            S.wait("vector", "DVE", d1)
            d2 = S.op("vector", lambda E: E.tensor_tensor(
                out=r_c[:, sl], in0=r_c[:, sl], in1=r_b[:, sl], op=ALU.subtract), "DVE")
            S.wait("scalar", "DVE", d2)
            a3 = S.op("scalar", lambda E: E.activation(
                out=r_b[:, sl], in_=r_c[:, sl], func=AF.Sqrt), "ACT")
            S.wait("vector", "ACT", a3)
            d3 = S.op("vector", lambda E: E.reciprocal(
                out=r_c[:, sl], in_=r_b[:, sl]), "DVE")
            S.wait("vector", "DVE", d3)
            d4 = S.op("vector", lambda E: E.tensor_tensor(
                out=r_b[:, sl], in0=r_a[:, sl], in1=r_c[:, sl], op=ALU.mult), "DVE")
            S.wait("tensor", "DVE", d4)
            pe = None
            for o in range(0, W, 512):
                w = min(512, W - o)
                S.op("tensor", lambda E, o=o, w=w: E.matmul(
                    p_rb[:, o:o + w], s_cmisc[0:1, 0:128],
                    r_c[:, roff + o:roff + o + w], start=True, stop=True), "PE")
                pe = S.op("tensor", lambda E, o=o, w=w: E.matmul(
                    p_bb[:, o:o + w], s_cmisc[0:1, 128:256],
                    r_b[:, roff + o:roff + o + w], start=True, stop=True), "PE")
            S.wait("vector", "PE", pe)
            S.op("vector", lambda E: E.tensor_copy(out=s_rstdb[:, 0:W], in_=p_rb), "DVE")
            dve = S.op("vector", lambda E: E.tensor_copy(out=s_bb[:, 0:W], in_=p_bb), "DVE")
            return dict(act=a3, dve=dve, d1=d1)

        with contextlib.ExitStack() as ph1:
            s_x = sb(ph1, "s_x", [128, 16 * 1024], F32)
            s_xln = sb(ph1, "s_xln", [128, 16 * 1024], BF16)
            s_sq1 = sb(ph1, "s_sq1", [128, 2 * 1024], BF16)
            s_t1 = sb(ph1, "s_t1", [128, 1024], F32)
            s_rstdb1 = sb(ph1, "s_rstdb1", [128, 1024], F32)
            s_bb1 = sb(ph1, "s_bb1", [128, 1024], F32)
            p_sm = ps(ph1, "p_sm", [1, 1024])
            p_ssq = ps(ph1, "p_ssq", [1, 1024])
            p_rb = ps(ph1, "p_rb", [128, 1024])
            p_bb = ps(ph1, "p_bb", [128, 1024])

            xload = []
            for c in range(16):
                xload.append(S.dma("sync", lambda E, c=c: E.dma_start(
                    out=s_x[:, c * 1024:(c + 1) * 1024],
                    in_=xt[c * 128:(c + 1) * 128, :])))

            mm_done = [0, 0]
            pe_last = None
            for c in range(16):
                par = c % 2
                xc = s_x[:, c * 1024:(c + 1) * 1024]
                sqc = s_sq1[:, par * 1024:(par + 1) * 1024]
                S.wait("scalar", "IN", xload[c])
                S.wait("scalar", "PE", mm_done[par])
                sq = S.op("scalar", lambda E, xc=xc, sqc=sqc: E.activation(
                    out=sqc, in_=xc, func=AF.Square), "ACT")
                S.wait("tensor", "ACT", sq)
                S.wait("tensor", "IN", xload[c])
                for o in (0, 512):
                    S.op("tensor", lambda E, xc=xc, o=o, c=c: E.matmul(
                        p_sm[:, o:o + 512], s_col32, xc[:, o:o + 512],
                        start=(c == 0), stop=(c == 15)), "PE")
                    pe_last = S.op("tensor", lambda E, sqc=sqc, o=o, c=c: E.matmul(
                        p_ssq[:, o:o + 512], s_colb, sqc[:, o:o + 512],
                        start=(c == 0), stop=(c == 15)), "PE")
                mm_done[par] = pe_last

            g1 = emit_ln_rows(p_sm, p_ssq, 1024, 0, p_rb, p_bb, s_rstdb1, s_bb1, pe_last)

            S.wait("vector", "DVE", g1["dve"])
            dve_norm = None
            for c in range(16):
                xc = s_x[:, c * 1024:(c + 1) * 1024]
                xlc = s_xln[:, c * 1024:(c + 1) * 1024]
                d1 = S.op("vector", lambda E, xc=xc: E.tensor_tensor(
                    out=s_t1, in0=xc, in1=s_rstdb1, op=ALU.mult), "DVE")
                S.wait("vector", "DVE", d1)
                dve_norm = S.op("vector", lambda E, xlc=xlc: E.tensor_tensor(
                    out=xlc, in0=s_t1, in1=s_bb1, op=ALU.add), "DVE")

            S.wait("scalar", "DVE", dve_norm)
            so_xln = S.dma("scalar", lambda E: E.dma_start(
                out=cc1_in.rearrange("(c p) t -> p c t", p=128),
                in_=s_xln.rearrange("p (c t) -> p c t", c=16)), "SO")

            S.wait("gpsimd", "SO", so_xln)
            cc1 = S.op("gpsimd", lambda E: E.collective_compute(
                "AllGather", ALU.bypass, replica_groups=RG,
                ins=[cc1_in.opt()], outs=[cc1_out.opt()]), "CC")

            if dbg_stop == 1:
                d_xln = dbg("d_xln", [D, TC], BF16)
                d_g = dbg("d_g", [NCORES * D, TC], BF16)
                S.wait("gpsimd", "CC", cc1)
                S.dma("gpsimd", lambda E: E.dma_start(
                    out=d_xln.rearrange("(c p) t -> p c t", p=128),
                    in_=s_xln.rearrange("p (c t) -> p c t", c=16)), "SO")
                S.dma("gpsimd", lambda E: E.dma_start(out=d_g, in_=cc1_out), "SO")
                S.wait("gpsimd", "SO", S.n["SO"])
                return _finish(nc, S, sems, root)

        S.barrier()

        qrow = 0
        att = root.enter_context(contextlib.ExitStack())
        s_kt = sb(att, "s_kt", [65, T], BF16)
        s_vt = sb(att, "s_vt", [128, 64 * 65], BF16)
        s_qt = sb(att, "s_qt", [65, NQH * T], BF16)

        S.op("vector", lambda E: E.memset(s_kt[64:65, :], 1.0), "DVE")
        vt_ones = s_vt.rearrange("p (j f) -> p j f", f=65)[:, :, 64:65]
        dve_ph0 = S.op("vector", lambda E: E.memset(vt_ones, 1.0), "DVE")

        with contextlib.ExitStack() as ph2:
            s_ctok = sb(ph2, "s_ctok", [128, 2048], F32)
            s_stok = sb(ph2, "s_stok", [128, 2048], F32)
            s_xg = sb(ph2, "s_xg", [128, 2 * 8192], BF16)
            s_rope = sb(ph2, "s_rope", [128, 2 * 384], BF16)
            s_r1 = sb(ph2, "s_r1", [128, 160], F32)
            s_r2 = sb(ph2, "s_r2", [128, 160], F32)
            s_r3 = sb(ph2, "s_r3", [128, 160], F32)
            s_r4 = sb(ph2, "s_r4", [128, 160], F32)
            s_sqk = sb(ph2, "s_sqk", [64, 2 * 512], BF16)
            s_tred = sb(ph2, "s_tred", [65, 2], F32)
            s_brow = sb(ph2, "s_brow", [65, 1024], F32)
            p_qkv0 = ps(ph2, "p_qkv0", [128, 384])
            p_qkv1 = ps(ph2, "p_qkv1", [128, 384])
            p_tr0 = ps(ph2, "p_tr0", [64, 128], BF16)
            p_tr1 = ps(ph2, "p_tr1", [64, 128], BF16)
            p_nrm = ps(ph2, "p_nrm", [65, 1024])
            p_qkv = (p_qkv0, p_qkv1)
            p_tr = (p_tr0, p_tr1)

            S.dma("sync", lambda E: E.dma_start(out=s_ctok, in_=ctokd))
            tabs = S.dma("sync", lambda E: E.dma_start(out=s_stok, in_=stokd))

            xg_load = [0, 0]
            xg_cons = [0, 0]
            qkv_cons = [0, 0]
            tr_evac = [0, 0]
            rope_done_t = {}
            last_tr_pe = {}
            pending_tr = None

            def emit_transposes(tt):
                rp = (tt % 2) * 384
                for j in range(5):
                    part = (tt * 5 + j) % 2
                    S.wait("tensor", "DVE", rope_done_t[tt])
                    S.wait("tensor", "DVE", tr_evac[part])
                    pe_t = S.op("tensor", lambda E, part=part, j=j, rp=rp: E.transpose(
                        p_tr[part], s_rope[:, rp + j * 64: rp + (j + 1) * 64],
                        s_ident), "PE")
                    last_tr_pe[tt] = pe_t
                    S.wait("vector", "PE", pe_t)
                    if j < 4:
                        dst = s_qt[0:64, j * T + tt * 128: j * T + tt * 128 + 128]
                    else:
                        dst = s_kt[0:64, tt * 128: tt * 128 + 128]
                    tr_evac[part] = S.op(
                        "vector", lambda E, part=part, dst=dst: E.tensor_scalar(
                            out=dst, in0=p_tr[part][0:64, :], scalar1=SC, scalar2=None,
                            op0=ALU.mult), "DVE")

            S.wait("sync", "CC", 1)
            for b in range(16):
                par = b % 2
                r, hb = b // 2, b % 2
                S.wait("sync", "PE", xg_cons[par])
                xg_load[par] = S.dma("sync", lambda E, par=par, r=r, hb=hb: E.dma_start(
                    out=s_xg[:, par * 8192:(par + 1) * 8192]
                    .rearrange("p (c t) -> p c t", c=16),
                    in_=cc1_out[r * D:(r + 1) * D, hb * 512:(hb + 1) * 512]
                    .rearrange("(c p) t -> p c t", p=128)))
                for t4 in range(4):
                    tt = b * 4 + t4
                    parp = tt % 2
                    S.wait("tensor", "IN", xg_load[par])
                    S.wait("tensor", "WL", wl_wqkv)
                    S.wait("tensor", "DVE", qkv_cons[parp])
                    pe_p = None
                    for c in range(16):
                        pe_p = S.op("tensor", lambda E, par=par, parp=parp, c=c, t4=t4: E.matmul(
                            p_qkv[parp],
                            s_xg[:, par * 8192 + c * 512 + t4 * 128:
                                 par * 8192 + c * 512 + t4 * 128 + 128],
                            s_wqkv[:, c * 384:(c + 1) * 384],
                            start=(c == 0), stop=(c == 15)), "PE")
                    xg_cons[par] = pe_p
                    if pending_tr is not None:
                        emit_transposes(pending_tr)
                    pq = p_qkv[parp]
                    x1 = pq[:, 0:320].rearrange("p (h x f) -> p h x f", h=5, x=2)[:, :, 0, :]
                    x2v = pq[:, 0:320].rearrange("p (h x f) -> p h x f", h=5, x=2)[:, :, 1, :]
                    ctb = s_ctok[:, tt * 32:(tt + 1) * 32].unsqueeze(1).broadcast_to([128, 5, 32])
                    stb = s_stok[:, tt * 32:(tt + 1) * 32].unsqueeze(1).broadcast_to([128, 5, 32])
                    rp = (tt % 2) * 384
                    rview = s_rope[:, rp:rp + 320].rearrange("p (h x f) -> p h x f", h=5, x=2)
                    o1, o2 = rview[:, :, 0, :], rview[:, :, 1, :]
                    r1v = s_r1.rearrange("p (h f) -> p h f", h=5)
                    r2v = s_r2.rearrange("p (h f) -> p h f", h=5)
                    r3v = s_r3.rearrange("p (h f) -> p h f", h=5)
                    r4v = s_r4.rearrange("p (h f) -> p h f", h=5)
                    S.wait("vector", "IN", tabs)
                    S.wait("vector", "PE", pe_p)
                    S.wait("vector", "PE", last_tr_pe.get(tt - 2, 0))
                    S.op("vector", lambda E, x1=x1, ctb=ctb, r1v=r1v: E.tensor_tensor(
                        out=r1v, in0=x1, in1=ctb, op=ALU.mult), "DVE")
                    S.op("vector", lambda E, x2v=x2v, stb=stb, r2v=r2v: E.tensor_tensor(
                        out=r2v, in0=x2v, in1=stb, op=ALU.mult), "DVE")
                    S.op("vector", lambda E, x1=x1, stb=stb, r3v=r3v: E.tensor_tensor(
                        out=r3v, in0=x1, in1=stb, op=ALU.mult), "DVE")
                    d4 = S.op("vector", lambda E, x2v=x2v, ctb=ctb, r4v=r4v: E.tensor_tensor(
                        out=r4v, in0=x2v, in1=ctb, op=ALU.mult), "DVE")
                    S.wait("vector", "DVE", d4)
                    S.op("vector", lambda E, o1=o1, r1v=r1v, r2v=r2v: E.tensor_tensor(
                        out=o1, in0=r1v, in1=r2v, op=ALU.subtract), "DVE")
                    rope_done_t[tt] = S.op(
                        "vector", lambda E, o2=o2, r3v=r3v, r4v=r4v: E.tensor_tensor(
                            out=o2, in0=r3v, in1=r4v, op=ALU.add), "DVE")
                    qkv_cons[parp] = S.op("vector", lambda E, pq=pq, tt=tt: E.tensor_copy(
                        out=s_vt[:, tt * 65: tt * 65 + 64], in_=pq[:, 320:384]), "DVE")
                    pending_tr = tt
            emit_transposes(pending_tr)

            all_evac = S.n["DVE"]
            kq_pe = [0, 0]
            for s in range(16):
                par = s % 2
                sqv = s_sqk[0:64, par * 512:(par + 1) * 512]
                S.wait("scalar", "DVE", all_evac)
                S.wait("scalar", "PE", kq_pe[par])
                a = S.op("scalar", lambda E, s=s, sqv=sqv: E.activation(
                    out=sqv, in_=s_kt[0:64, s * 512:(s + 1) * 512], func=AF.Square), "ACT")
                S.wait("tensor", "ACT", a)
                S.wait("tensor", "DVE", S.n["DVE"])
                kq_pe[par] = S.op("tensor", lambda E, par=par, sqv=sqv: E.matmul(
                    p_nrm[64:65, par * 512:(par + 1) * 512], s_colb[0:64, :], sqv,
                    start=True, stop=True), "PE")
                S.wait("vector", "PE", kq_pe[par])
                d = S.op("vector", lambda E, par=par: E.tensor_reduce(
                    out=s_tred[64:65, par:par + 1],
                    in_=p_nrm[64:65, par * 512:(par + 1) * 512],
                    axis=AX.X, op=ALU.max), "DVE")
                S.wait("vector", "DVE", d)
                S.op("vector", lambda E, par=par: E.tensor_tensor(
                    out=s_km2[64:65, :], in0=s_km2[64:65, :],
                    in1=s_tred[64:65, par:par + 1], op=ALU.max), "DVE")

            km_done = S.n["DVE"]
            for h in range(NQH):
                for s in range(16):
                    par = s % 2
                    off = h * T + s * 512
                    sqv = s_sqk[0:64, par * 512:(par + 1) * 512]
                    S.wait("scalar", "DVE", km_done)
                    S.wait("scalar", "PE", kq_pe[par])
                    a = S.op("scalar", lambda E, off=off, sqv=sqv: E.activation(
                        out=sqv, in_=s_qt[0:64, off:off + 512], func=AF.Square), "ACT")
                    S.wait("tensor", "ACT", a)
                    S.wait("tensor", "DVE", S.n["DVE"])
                    kq_pe[par] = S.op("tensor", lambda E, par=par, sqv=sqv: E.matmul(
                        p_nrm[64:65, par * 512:(par + 1) * 512], s_colb[0:64, :], sqv,
                        start=True, stop=True), "PE")
                    S.wait("vector", "PE", kq_pe[par])
                    dprod = S.op("vector", lambda E, par=par: E.tensor_scalar(
                        out=s_brow[64:65, 0:512],
                        in0=p_nrm[64:65, par * 512:(par + 1) * 512],
                        scalar1=s_km2[64:65, 0:1], scalar2=None,
                        op0=ALU.mult), "DVE")
                    S.wait("scalar", "DVE", dprod)
                    asq = S.op("scalar", lambda E: E.activation(
                        out=s_brow[64:65, 512:1024], in_=s_brow[64:65, 0:512],
                        func=AF.Sqrt), "ACT")
                    S.wait("vector", "ACT", asq)
                    qrow = S.op("vector", lambda E, off=off: E.tensor_scalar(
                        out=s_qt[64:65, off:off + 512],
                        in0=s_brow[64:65, 512:1024],
                        scalar1=-1.0, scalar2=None, op0=ALU.mult), "DVE")

            if dbg_stop == 3:
                d_qt = dbg("d_qt", [65, NQH * T], BF16)
                d_kt = dbg("d_kt", [65, T], BF16)
                d_vt = dbg("d_vt", [128, 64 * 65], BF16)
                S.wait("gpsimd", "DVE", qrow)
                S.dma("gpsimd", lambda E: E.dma_start(out=d_qt, in_=s_qt), "SO")
                S.dma("gpsimd", lambda E: E.dma_start(out=d_kt, in_=s_kt), "SO")
                S.dma("gpsimd", lambda E: E.dma_start(out=d_vt, in_=s_vt), "SO")
                S.wait("gpsimd", "SO", S.n["SO"])
                return _finish(nc, S, sems, root)

        s_wo_cm = nc.sbuf_tensor("s_wo", [128, 16 * 1024], BF16, side="right")
        s_wo = s_wo_cm.__enter__()[:, :]
        S.wait("gpsimd", "DVE", qrow)
        wo_done = None
        for ac in range(16):
            wo_done = S.dma("gpsimd", lambda E, ac=ac: E.dma_start(
                out=s_wo[:, ac * 1024:(ac + 1) * 1024],
                in_=wot[ac * 128:(ac + 1) * 128, 0:1024]), "WL")

        S.barrier()

        with contextlib.ExitStack() as ph4:
            s_masks = sb(ph4, "s_masks", [128, 8 * 1024], BF16)
            s_p0 = sb(ph4, "s_p0", [128, 1024], BF16)
            s_p1 = sb(ph4, "s_p1", [128, 1024], BF16)
            s_p2 = sb(ph4, "s_p2", [128, 1024], BF16)
            s_bc = sb(ph4, "s_bc", [64, 1024], F32)
            s_rec = sb(ph4, "s_rec", [65, 1024], F32)
            s_rec2 = sb(ph4, "s_rec2", [65, 1024], F32)
            s_st0 = sb(ph4, "s_st0", [64, 1024], BF16)
            s_st1 = sb(ph4, "s_st1", [64, 1024], BF16)
            p_sc0 = ps(ph4, "p_sc0", [128, 1024])
            p_sc1 = ps(ph4, "p_sc1", [128, 1024])
            p_pv = ps(ph4, "p_pv", [65, 1024])
            p_bc4 = ps(ph4, "p_bc4", [64, 1024])
            p_sc = (p_sc0, p_sc1)
            s_p = (s_p0, s_p1, s_p2)
            s_st = (s_st0, s_st1)

            mload = S.dma("sync", lambda E: E.dma_start(out=s_masks, in_=masksd))

            S.wait("tensor", "DVE", qrow)
            S.wait("tensor", "DVE", dve_ph0)

            u = 0
            exp_cnt = {}
            score_cnt = {}
            pv_cnt = {}
            mask_cnt = {}
            stage_dve = 0
            stage_so = [0, 0]
            e_idx = 0
            pending_pv = None

            def emit_pv(uu, jj, jmax):
                S.wait("tensor", "ACT", exp_cnt[uu])
                if jj == 0:
                    S.wait("tensor", "DVE", stage_dve)
                last = None
                for o in (0, 512):
                    last = S.op("tensor", lambda E, jj=jj, uu=uu, o=o, jmax=jmax: E.matmul(
                        p_pv[:, o:o + 512], s_vt[:, jj * 65:(jj + 1) * 65],
                        s_p[uu % 3][:, o:o + 512],
                        start=(jj == 0), stop=(jj == jmax - 1)), "PE")
                pv_cnt[uu] = last

            for h in range(NQH):
                for qc in range(8):
                    jmax = 8 * qc + 8
                    for j in range(jmax):
                        par = u % 2
                        diag = j >= jmax - 8
                        S.wait("tensor", "ACT", exp_cnt.get(u - 2, 0))
                        for o in (0, 512):
                            score_cnt[u] = S.op(
                                "tensor", lambda E, par=par, j=j, h=h, qc=qc, o=o: E.matmul(
                                    p_sc[par][:, o:o + 512],
                                    s_kt[:, j * 128:(j + 1) * 128],
                                    s_qt[:, h * T + qc * 1024 + o:
                                         h * T + qc * 1024 + o + 512],
                                    start=True, stop=True), "PE")
                        if pending_pv is not None:
                            emit_pv(*pending_pv)
                        if diag:
                            mo = (j - (jmax - 8)) * 1024
                            S.wait("vector", "IN", mload)
                            S.wait("vector", "PE", score_cnt[u])
                            mask_cnt[u] = S.op(
                                "vector", lambda E, par=par, mo=mo: E.tensor_tensor(
                                    out=p_sc[par], in0=p_sc[par],
                                    in1=s_masks[:, mo:mo + 1024], op=ALU.add), "DVE")
                        S.wait("scalar", "PE", score_cnt[u])
                        if diag:
                            S.wait("scalar", "DVE", mask_cnt[u])
                        S.wait("scalar", "PE", pv_cnt.get(u - 3, 0))
                        exp_cnt[u] = S.op("scalar", lambda E, par=par, u=u: E.activation(
                            out=s_p[u % 3], in_=p_sc[par], func=AF.Exp), "ACT")
                        pending_pv = (u, j, jmax)
                        u += 1
                    emit_pv(*pending_pv)
                    pending_pv = None
                    par_e = e_idx % 2
                    S.wait("vector", "PE", pv_cnt[u - 1])
                    dcl = S.op("vector", lambda E: E.tensor_scalar(
                        out=s_rec2[64:65, :], in0=p_pv[64:65, :],
                        scalar1=1e-35, scalar2=None, op0=ALU.max), "DVE")
                    S.wait("vector", "DVE", dcl)
                    dr = S.op("vector", lambda E: E.reciprocal(
                        out=s_rec[64:65, :], in_=s_rec2[64:65, :]), "DVE")
                    S.wait("tensor", "DVE", dr)
                    pb = None
                    for o in (0, 512):
                        pb = S.op("tensor", lambda E, o=o: E.matmul(
                            p_bc4[:, o:o + 512], s_cmisc[64:65, 0:64],
                            s_rec[64:65, o:o + 512], start=True, stop=True), "PE")
                    S.wait("vector", "PE", pb)
                    de = S.op("vector", lambda E: E.tensor_copy(out=s_bc, in_=p_bc4), "DVE")
                    S.wait("vector", "DVE", de)
                    S.wait("vector", "SO", stage_so[par_e])
                    stage_dve = S.op("vector", lambda E, par_e=par_e: E.tensor_tensor(
                        out=s_st[par_e], in0=p_pv[0:64, :], in1=s_bc, op=ALU.mult), "DVE")
                    S.wait("scalar", "DVE", stage_dve)
                    stage_so[par_e] = S.dma(
                        "scalar", lambda E, par_e=par_e, h=h, qc=qc: E.dma_start(
                            out=cc2_in[h * 64:(h + 1) * 64, qc * 1024:(qc + 1) * 1024],
                            in_=s_st[par_e]), "SO")
                    e_idx += 1

            S.wait("gpsimd", "SO", S.n["SO"])
            cc2 = S.op("gpsimd", lambda E: E.collective_compute(
                "AllGather", ALU.bypass, replica_groups=RG,
                ins=[cc2_in.opt()], outs=[cc2_out.opt()]), "CC")

            if dbg_stop == 4:
                d_at = dbg("d_at", [NCORES * NQH * HD, T], BF16)
                d_ai = dbg("d_ai", [NQH * HD, T], BF16)
                S.wait("gpsimd", "CC", cc2)
                S.dma("gpsimd", lambda E: E.dma_start(out=d_at, in_=cc2_out), "SO")
                S.dma("gpsimd", lambda E: E.dma_start(out=d_ai, in_=cc2_in), "SO")
                S.wait("gpsimd", "SO", S.n["SO"])
                return _finish(nc, S, sems, root)

        att.close()
        S.barrier()

        with contextlib.ExitStack() as ph5:
            s_x2 = sb(ph5, "s_x2", [128, 16 * 1024], F32)
            s_attno = sb(ph5, "s_attno", [128, 16 * 1024], BF16)
            s_sq5 = sb(ph5, "s_sq5", [128, 2 * 512], BF16)
            s_xt5 = sb(ph5, "s_xt5", [128, 2 * 1024], F32)
            s_t5 = sb(ph5, "s_t5", [128, 512], F32)
            s_rstdb5 = sb(ph5, "s_rstdb5", [128, 2 * 512], F32)
            s_bb5 = sb(ph5, "s_bb5", [128, 2 * 512], F32)
            p_wo0 = ps(ph5, "p_wo0", [128, 512])
            p_wo1 = ps(ph5, "p_wo1", [128, 512])
            p_sm5 = ps(ph5, "p_sm5", [1, 1024])
            p_ssq5 = ps(ph5, "p_ssq5", [1, 1024])
            p_rb5 = ps(ph5, "p_rb5", [128, 512])
            p_bb5p = ps(ph5, "p_bb5p", [128, 512])
            p_wo = (p_wo0, p_wo1)

            S.wait("gpsimd", "CC", cc2)

            def attno_loads(E, sem):
                with E.register("rk") as rk:
                    E.cc_rank_ld(rk, replica_groups=RG)
                    with E.register("offr") as offr:
                        E.reg_mul(offr, rk, TC)
                        off = E.snap(offr, min_val=0, max_val=(NCORES - 1) * TC)
                        for ac in range(16):
                            E.dma_start(
                                out=s_attno[:, ac * 1024:(ac + 1) * 1024],
                                in_=cc2_out[ac * 128:(ac + 1) * 128,
                                            bass.ds(off, TC)]).then_inc(sem, 16)

            at_load = S.multi("gpsimd", attno_loads, "WL", 16 * 16)

            evac5 = [0, 0]
            sum_pe = None
            half0_pe = None
            pidx = 0
            for oh in range(2):
                if oh == 1:
                    S.wait("gpsimd", "PE", half0_pe)
                    for ac in range(16):
                        wo_done = S.dma("gpsimd", lambda E, ac=ac: E.dma_start(
                            out=s_wo[:, ac * 1024:(ac + 1) * 1024],
                            in_=wot[ac * 128:(ac + 1) * 128, 1024:2048]), "WL")
                for qh in range(2):
                    for ot in range(8):
                        og = oh * 8 + ot
                        par = pidx % 2
                        pidx += 1
                        S.wait("tensor", "WL", wo_done)
                        S.wait("tensor", "WL", at_load)
                        S.wait("tensor", "ACT", evac5[par])
                        mm = None
                        for ac in range(16):
                            mm = S.op("tensor",
                                      lambda E, par=par, ac=ac, ot=ot, qh=qh: E.matmul(
                                          p_wo[par],
                                          s_wo[:, ac * 1024 + ot * 128:
                                               ac * 1024 + ot * 128 + 128],
                                          s_attno[:, ac * 1024 + qh * 512:
                                                  ac * 1024 + qh * 512 + 512],
                                          start=(ac == 0), stop=(ac == 15)), "PE")
                        dst = s_x2[:, og * 1024 + qh * 512: og * 1024 + qh * 512 + 512]
                        sqv = s_sq5[:, par * 512:(par + 1) * 512]
                        S.wait("scalar", "PE", mm)
                        S.op("scalar", lambda E, par=par, dst=dst: E.activation(
                            out=dst, in_=p_wo[par], func=AF.Copy), "ACT")
                        evac5[par] = S.op("scalar", lambda E, par=par, sqv=sqv: E.activation(
                            out=sqv, in_=p_wo[par], func=AF.Square), "ACT")
                        S.wait("tensor", "ACT", evac5[par])
                        S.op("tensor", lambda E, qh=qh, dst=dst, og=og: E.matmul(
                            p_sm5[:, qh * 512:(qh + 1) * 512], s_col32, dst,
                            start=(og == 0), stop=(og == 15)), "PE")
                        sum_pe = S.op("tensor", lambda E, qh=qh, sqv=sqv, og=og: E.matmul(
                            p_ssq5[:, qh * 512:(qh + 1) * 512], s_colb, sqv,
                            start=(og == 0), stop=(og == 15)), "PE")
                if oh == 0:
                    half0_pe = S.n["PE"]

            g5 = None
            for qh in range(2):
                g5 = emit_ln_rows(
                    p_sm5[:, qh * 512:(qh + 1) * 512],
                    p_ssq5[:, qh * 512:(qh + 1) * 512],
                    512, qh * 512, p_rb5, p_bb5p,
                    s_rstdb5[:, qh * 512:(qh + 1) * 512],
                    s_bb5[:, qh * 512:(qh + 1) * 512], sum_pe)

            S.wait("vector", "DVE", g5["dve"])
            xt5_cons = [0, 0]
            xload5 = [0, 0]
            x2_dve = None
            for c in range(16):
                par = c % 2
                S.wait("sync", "DVE", xt5_cons[par])
                xload5[par] = S.dma("sync", lambda E, c=c, par=par: E.dma_start(
                    out=s_xt5[:, par * 1024:(par + 1) * 1024],
                    in_=xt[c * 128:(c + 1) * 128, :]))
                for qh in range(2):
                    xc = s_x2[:, c * 1024 + qh * 512: c * 1024 + qh * 512 + 512]
                    d1 = S.op("vector", lambda E, xc=xc, qh=qh: E.tensor_tensor(
                        out=s_t5, in0=xc, in1=s_rstdb5[:, qh * 512:(qh + 1) * 512],
                        op=ALU.mult), "DVE")
                    S.wait("vector", "DVE", d1)
                    d2 = S.op("vector", lambda E, qh=qh: E.tensor_tensor(
                        out=s_t5, in0=s_t5, in1=s_bb5[:, qh * 512:(qh + 1) * 512],
                        op=ALU.add), "DVE")
                    S.wait("vector", "DVE", d2)
                    S.wait("vector", "IN", xload5[par])
                    x2_dve = S.op("vector", lambda E, xc=xc, par=par, qh=qh: E.tensor_tensor(
                        out=xc, in0=s_t5,
                        in1=s_xt5[:, par * 1024 + qh * 512: par * 1024 + qh * 512 + 512],
                        op=ALU.add), "DVE")
                    xt5_cons[par] = x2_dve

            S.wait("scalar", "DVE", x2_dve)
            so_x2 = S.dma("scalar", lambda E: E.dma_start(
                out=x2d.rearrange("(c p) t -> p c t", p=128),
                in_=s_x2.rearrange("p (c t) -> p c t", c=16)), "SO")

            if dbg_stop == 5:
                d_x2 = dbg("d_x2", [D, TC], F32)
                S.wait("gpsimd", "SO", so_x2)
                S.dma("gpsimd", lambda E: E.dma_start(out=d_x2, in_=x2d), "SO")
                S.wait("gpsimd", "SO", S.n["SO"])
                return _finish(nc, S, sems, root)

        s_wo_cm.__exit__(None, None, None)
        S.barrier()

        y_so = [0, 0]
        for qh in range(2):
            S.barrier()
            with contextlib.ExitStack() as ph6:
                s_fuse = sb(ph6, "s_fuse", [128, 64 * 512], BF16)
                s_sq6 = sb(ph6, "s_sq6", [128, 2 * 512], BF16)
                s_x2s = sb(ph6, "s_x2s", [128, 2 * 512], F32)
                s_t6 = sb(ph6, "s_t6", [128, 512], F32)
                s_rstdb6 = sb(ph6, "s_rstdb6", [128, 512], F32)
                s_bb6 = sb(ph6, "s_bb6", [128, 512], F32)
                s_y0 = sb(ph6, "s_y0", [128, 512], F32)
                s_y1 = sb(ph6, "s_y1", [128, 512], F32)
                p_sm6 = ps(ph6, "p_sm6", [1, 512])
                p_ssq6 = ps(ph6, "p_ssq6", [1, 512])
                p_rb6 = ps(ph6, "p_rb6", [128, 512])
                p_bb6p = ps(ph6, "p_bb6p", [128, 512])
                s_y = (s_y0, s_y1)

                def x2_chunk(c):
                    return x2d[c * 128:(c + 1) * 128, qh * 512:(qh + 1) * 512]

                S.wait("sync", "SO", so_x2)
                sq_pe = [0, 0]
                x2_cons = [0, 0]
                mm_last = None
                for c in range(16):
                    par = c % 2
                    xv = s_x2s[:, par * 512:(par + 1) * 512]
                    sqv = s_sq6[:, par * 512:(par + 1) * 512]
                    src_c = x2_chunk(c)
                    S.wait("sync", "PE", x2_cons[par])
                    ld = S.dma("sync", lambda E, src_c=src_c, xv=xv: E.dma_start(
                        out=xv, in_=src_c))
                    S.wait("scalar", "IN", ld)
                    S.wait("scalar", "PE", sq_pe[par])
                    a = S.op("scalar", lambda E, xv=xv, sqv=sqv: E.activation(
                        out=sqv, in_=xv, func=AF.Square), "ACT")
                    S.wait("tensor", "ACT", a)
                    S.wait("tensor", "IN", ld)
                    S.op("tensor", lambda E, xv=xv, c=c: E.matmul(
                        p_sm6, s_col32, xv, start=(c == 0), stop=(c == 15)), "PE")
                    mm_last = S.op("tensor", lambda E, sqv=sqv, c=c: E.matmul(
                        p_ssq6, s_colb, sqv, start=(c == 0), stop=(c == 15)), "PE")
                    sq_pe[par] = mm_last
                    x2_cons[par] = mm_last
                g3 = emit_ln_rows(p_sm6, p_ssq6, 512, 0, p_rb6, p_bb6p,
                                  s_rstdb6, s_bb6, mm_last)

                with contextlib.ExitStack() as ph6a:
                    s_xln3 = sb(ph6a, "s_xln3", [128, 16 * 512], BF16)
                    s_wgu0 = sb(ph6a, "s_wgu0", [128, 2 * 2048], BF16)
                    s_wgu1 = sb(ph6a, "s_wgu1", [128, 2 * 2048], BF16)
                    s_ga = sb(ph6a, "s_ga", [128, 2 * 512], F32)
                    s_ua = sb(ph6a, "s_ua", [128, 2 * 512], F32)
                    p_g0 = ps(ph6a, "p_g0", [128, 512])
                    p_g1 = ps(ph6a, "p_g1", [128, 512])
                    p_u0 = ps(ph6a, "p_u0", [128, 512])
                    p_u1 = ps(ph6a, "p_u1", [128, 512])
                    s_wgu = (s_wgu0, s_wgu1)
                    p_gu = ((p_g0, p_u0), (p_g1, p_u1))

                    S.wait("vector", "DVE", g3["dve"])
                    S.wait("sync", "PE", mm_last)
                    xln3_done = None
                    x2n_cons = [0, 0]
                    for c in range(16):
                        par = c % 2
                        xv = s_x2s[:, par * 512:(par + 1) * 512]
                        src_c = x2_chunk(c)
                        S.wait("sync", "DVE", x2n_cons[par])
                        ld = S.dma("sync", lambda E, src_c=src_c, xv=xv: E.dma_start(
                            out=xv, in_=src_c))
                        S.wait("vector", "IN", ld)
                        d1 = S.op("vector", lambda E, xv=xv: E.tensor_tensor(
                            out=s_t6, in0=xv, in1=s_rstdb6, op=ALU.mult), "DVE")
                        S.wait("vector", "DVE", d1)
                        xln3_done = S.op("vector", lambda E, c=c: E.tensor_tensor(
                            out=s_xln3[:, c * 512:(c + 1) * 512], in0=s_t6, in1=s_bb6,
                            op=ALU.add), "DVE")
                        x2n_cons[par] = xln3_done

                    gu_pe = [0, 0]
                    fuse_dve = [0, 0]
                    for it in range(64):
                        par = it % 2
                        S.wait("gpsimd", "PE", gu_pe[par])
                        S.dma("gpsimd", lambda E, it=it, par=par: E.dma_start(
                            out=s_wgu[par][:, 0:2048],
                            in_=wg_t[it * 128:(it + 1) * 128, :]), "WL")
                        wl = S.dma("gpsimd", lambda E, it=it, par=par: E.dma_start(
                            out=s_wgu[par][:, 2048:4096],
                            in_=wu_t[it * 128:(it + 1) * 128, :]), "WL")
                        S.wait("tensor", "WL", wl)
                        S.wait("tensor", "DVE", xln3_done)
                        S.wait("tensor", "DVE", fuse_dve[par])
                        mmg = None
                        for c in range(16):
                            S.op("tensor", lambda E, par=par, c=c: E.matmul(
                                p_gu[par][0], s_wgu[par][:, c * 128:(c + 1) * 128],
                                s_xln3[:, c * 512:(c + 1) * 512],
                                start=(c == 0), stop=(c == 15)), "PE")
                            mmg = S.op("tensor", lambda E, par=par, c=c: E.matmul(
                                p_gu[par][1],
                                s_wgu[par][:, 2048 + c * 128: 2048 + c * 128 + 128],
                                s_xln3[:, c * 512:(c + 1) * 512],
                                start=(c == 0), stop=(c == 15)), "PE")
                        gu_pe[par] = mmg
                        gav = s_ga[:, par * 512:(par + 1) * 512]
                        uav = s_ua[:, par * 512:(par + 1) * 512]
                        S.wait("scalar", "PE", mmg)
                        S.wait("scalar", "DVE", fuse_dve[par])
                        S.op("scalar", lambda E, par=par, it=it, gav=gav: E.activation(
                            out=gav, in_=p_gu[par][0], func=AF.Silu,
                            scale=s_sg[:, it:it + 1]), "ACT")
                        ga = S.op("scalar", lambda E, par=par, it=it, uav=uav: E.activation(
                            out=uav, in_=p_gu[par][1], func=AF.Copy,
                            scale=s_su[:, it:it + 1]), "ACT")
                        S.wait("vector", "ACT", ga)
                        fuse_dve[par] = S.op(
                            "vector", lambda E, it=it, gav=gav, uav=uav: E.tensor_tensor(
                                out=s_fuse[:, it * 512:(it + 1) * 512], in0=gav, in1=uav,
                                op=ALU.mult), "DVE")
                    fuse_all = S.n["DVE"]
                    gu_pe_all = S.n["PE"]
                    gu_act_all = S.n["ACT"]
                    if dbg_stop == 60 and qh == 0:
                        d_xln3 = dbg("d_xln3", [128, 16 * 512], BF16)
                        d_fuse = dbg("d_fuse", [128, 64 * 512], BF16)
                        d_x2q = dbg("d_x2q", [D, TC], F32)
                        S.wait("gpsimd", "DVE", fuse_all)
                        S.dma("gpsimd", lambda E: E.dma_start(out=d_x2q, in_=x2d), "SO")
                        S.dma("gpsimd", lambda E: E.dma_start(out=d_xln3, in_=s_xln3), "SO")
                        S.dma("gpsimd", lambda E: E.dma_start(out=d_fuse, in_=s_fuse), "SO")
                        S.wait("gpsimd", "SO", S.n["SO"])
                        return _finish(nc, S, sems, root)

                with contextlib.ExitStack() as ph6b:
                    s_wdh0 = sb(ph6b, "s_wdh0", [128, 32 * 512], BF16)
                    s_wdh1 = sb(ph6b, "s_wdh1", [128, 32 * 512], BF16)
                    s_ffno = sb(ph6b, "s_ffno", [128, 16 * 512], F32)
                    p_d0 = ps(ph6b, "p_d0", [128, 512])
                    p_d1 = ps(ph6b, "p_d1", [128, 512])
                    p_d2 = ps(ph6b, "p_d2", [128, 512])
                    p_d3 = ps(ph6b, "p_d3", [128, 512])
                    s_wdh = (s_wdh0, s_wdh1)
                    p_d = (p_d0, p_d1, p_d2, p_d3)

                    S.wait("gpsimd", "DVE", fuse_all)
                    wd_pe = [gu_pe_all, gu_pe_all]
                    evac6_guard = gu_act_all
                    sq6_pe = [0, 0]
                    sum_pe6 = None
                    ln3_guarded = False
                    for g4 in range(4):
                        wd_wl = [0, 0]
                        for half in range(2):
                            S.wait("gpsimd", "PE", wd_pe[half])
                            wd_wl[half] = S.dma(
                                "gpsimd", lambda E, g4=g4, half=half: E.dma_start(
                                    out=s_wdh[half],
                                    in_=wd_t[g4 * 128:(g4 + 1) * 128,
                                             half * 16384:(half + 1) * 16384],
                                    max_dma_last_dim=2048), "WL")
                        S.wait("tensor", "DVE", fuse_all)
                        S.wait("tensor", "ACT", evac6_guard)
                        dn_mm = None
                        for ic in range(64):
                            half = ic // 32
                            S.wait("tensor", "WL", wd_wl[half])
                            for ot2 in range(4):
                                dn_mm = S.op(
                                    "tensor", lambda E, half=half, ic=ic, ot2=ot2: E.matmul(
                                        p_d[ot2],
                                        s_wdh[half][:, (ic % 32) * 512 + ot2 * 128:
                                                    (ic % 32) * 512 + ot2 * 128 + 128],
                                        s_fuse[:, ic * 512:(ic + 1) * 512],
                                        start=(ic == 0), stop=(ic == 63)), "PE")
                            if ic == 31:
                                wd_pe[0] = dn_mm
                        wd_pe[1] = dn_mm
                        for ot2 in range(4):
                            og = g4 * 4 + ot2
                            par = og % 2
                            dst = s_ffno[:, og * 512:(og + 1) * 512]
                            sqv = s_sq6[:, par * 512:(par + 1) * 512]
                            S.wait("scalar", "PE", dn_mm)
                            S.wait("scalar", "PE", sq6_pe[par])
                            S.op("scalar", lambda E, ot2=ot2, og=og, dst=dst: E.activation(
                                out=dst, in_=p_d[ot2], func=AF.Copy,
                                scale=s_sd[:, og:og + 1]), "ACT")
                            ev = S.op("scalar", lambda E, ot2=ot2, og=og, sqv=sqv: E.activation(
                                out=sqv, in_=p_d[ot2], func=AF.Square,
                                scale=s_sd[:, og:og + 1]), "ACT")
                            evac6_guard = ev
                            S.wait("tensor", "ACT", ev)
                            if not ln3_guarded:
                                S.wait("tensor", "DVE", g3["d1"])
                                ln3_guarded = True
                            S.op("tensor", lambda E, og=og, dst=dst: E.matmul(
                                p_sm6, s_col32, dst, start=(og == 0), stop=(og == 15)), "PE")
                            sum_pe6 = S.op("tensor", lambda E, og=og, sqv=sqv: E.matmul(
                                p_ssq6, s_colb, sqv, start=(og == 0), stop=(og == 15)), "PE")
                            sq6_pe[par] = sum_pe6

                    if dbg_stop == 61 and qh == 0:
                        d_ffno = dbg("d_ffno", [128, 16 * 512], F32)
                        S.wait("gpsimd", "ACT", evac6_guard)
                        S.wait("gpsimd", "PE", sum_pe6)
                        S.dma("gpsimd", lambda E: E.dma_start(out=d_ffno, in_=s_ffno), "SO")
                        S.wait("gpsimd", "SO", S.n["SO"])
                        return _finish(nc, S, sems, root)
                    g4r = emit_ln_rows(p_sm6, p_ssq6, 512, 512, p_rb6, p_bb6p,
                                       s_rstdb6, s_bb6, sum_pe6)
                    S.wait("vector", "DVE", g4r["dve"])
                    S.wait("sync", "DVE", g4r["dve"])
                    x2f_cons = [0, 0]
                    for og in range(16):
                        par = og % 2
                        fv = s_ffno[:, og * 512:(og + 1) * 512]
                        xv = s_x2s[:, par * 512:(par + 1) * 512]
                        src_c = x2_chunk(og)
                        S.wait("sync", "DVE", x2f_cons[par])
                        ld = S.dma("sync", lambda E, src_c=src_c, xv=xv: E.dma_start(
                            out=xv, in_=src_c))
                        d1 = S.op("vector", lambda E, fv=fv: E.tensor_tensor(
                            out=s_t6, in0=fv, in1=s_rstdb6, op=ALU.mult), "DVE")
                        S.wait("vector", "DVE", d1)
                        d2 = S.op("vector", lambda E: E.tensor_tensor(
                            out=s_t6, in0=s_t6, in1=s_bb6, op=ALU.add), "DVE")
                        S.wait("vector", "DVE", d2)
                        S.wait("vector", "IN", ld)
                        S.wait("vector", "SO", y_so[par])
                        d3 = S.op("vector", lambda E, xv=xv, par=par: E.tensor_tensor(
                            out=s_y[par], in0=s_t6, in1=xv, op=ALU.add), "DVE")
                        x2f_cons[par] = d3
                        S.wait("scalar", "DVE", d3)
                        y_so[par] = S.dma(
                            "scalar", lambda E, par=par, og=og, qh=qh: E.dma_start(
                                out=y[og * 128:(og + 1) * 128, qh * 512:(qh + 1) * 512],
                                in_=s_y[par]), "SO")

        S.wait("scalar", "SO", S.n["SO"])

        return _finish(nc, S, sems, root)


def _finish(nc, S, sems, root):
    block = root.enter_context(nc.Block())
    S.emit(block, sems)
    return nc



def _perm_half(block):
    return np.concatenate([block[:, 0::2], block[:, 1::2]], axis=1)


def host_prep(L0_attn_x, L0_attn_Wq, L0_attn_Wk, L0_attn_Wv, L0_attn_Wo,
              L0_ffn_w_gate, L0_ffn_w_up, L0_ffn_w_down,
              L0_ffn_s_gate, L0_ffn_s_up, L0_ffn_s_down):
    f32 = np.float32
    xT = np.ascontiguousarray(np.asarray(L0_attn_x, dtype=f32)[0].T)
    WqT = np.asarray(L0_attn_Wq, dtype=f32).T
    WkT = np.asarray(L0_attn_Wk, dtype=f32).T
    WvT = np.asarray(L0_attn_Wv, dtype=f32).T
    woT = np.ascontiguousarray(np.asarray(L0_attn_Wo, dtype=f32).T)

    wg = np.asarray(L0_ffn_w_gate)
    wu = np.asarray(L0_ffn_w_up)
    wd = np.asarray(L0_ffn_w_down)
    wg_t = np.ascontiguousarray(
        wg.reshape(16, 128, 64, 128).transpose(2, 1, 0, 3).reshape(INTER, D))
    wu_t = np.ascontiguousarray(
        wu.reshape(16, 128, 64, 128).transpose(2, 1, 0, 3).reshape(INTER, D))
    wd_t = np.ascontiguousarray(
        wd.reshape(64, 128, 4, 512).transpose(2, 1, 0, 3).reshape(512, 64 * 512))

    sg = np.ascontiguousarray(np.asarray(L0_ffn_s_gate, dtype=f32).reshape(64, 128).T)
    su = np.ascontiguousarray(np.asarray(L0_ffn_s_up, dtype=f32).reshape(64, 128).T)
    sd = np.ascontiguousarray(np.asarray(L0_ffn_s_down, dtype=f32).reshape(16, 128).T)

    inv = 1.0 / 10000.0 ** (np.arange(0, HD, 2, dtype=np.float64) / HD)
    pos = np.arange(T, dtype=np.float64)
    f = pos[:, None] * inv[None, :]
    ctok = np.ascontiguousarray(
        np.cos(f).astype(f32).reshape(64, 128, 32).transpose(1, 0, 2).reshape(128, 2048))
    stok = np.ascontiguousarray(
        np.sin(f).astype(f32).reshape(64, 128, 32).transpose(1, 0, 2).reshape(128, 2048))

    l_i = np.arange(128)[:, None]
    q_i = np.arange(1024)[None, :]
    masks = np.concatenate(
        [np.where(m * 128 + l_i > q_i, -1e9, 0.0) for m in range(8)],
        axis=1).astype(ml_dtypes.bfloat16)

    ident = np.eye(128, dtype=ml_dtypes.bfloat16)
    cmisc = np.concatenate([np.ones((65, 128), f32), -np.ones((65, 128), f32)], axis=1)
    col32 = np.ones((128, 1), f32)
    colb = np.ones((128, 1), ml_dtypes.bfloat16)

    in_maps = []
    for c in range(NCORES):
        heads = []
        for j in range(NQH):
            h = 4 * c + j
            heads.append(_perm_half(WqT[:, 64 * h:64 * h + 64]))
        heads.append(_perm_half(WkT[:, 64 * c:64 * c + 64]))
        heads.append(WvT[:, 64 * c:64 * c + 64])
        wqkv_c = np.ascontiguousarray(np.concatenate(heads, axis=1))
        in_maps.append(dict(
            xt=np.ascontiguousarray(xT[:, TC * c:TC * (c + 1)]),
            wqkv=wqkv_c, wot=woT, wg_t=wg_t, wu_t=wu_t, wd_t=wd_t,
            sgd=sg, sud=su, sdd=sd, ctokd=ctok, stokd=stok, masksd=masks,
            identd=ident, cmiscd=cmisc, col32d=col32, colbd=colb,
        ))
    return in_maps


def kernel(**inputs):
    in_maps = host_prep(**inputs)
    nc = build_nc()
    res = run_bass_kernel_spmd(nc, in_maps, core_ids=list(range(NCORES)))
    out = np.empty((1, T, D), dtype=np.float32)
    for c in range(NCORES):
        out[0, TC * c:TC * (c + 1), :] = res.results[c]["y"].T
    return out


# version 04
# speedup vs baseline: 1.1324x; 1.1324x over previous
import contextlib
import numpy as np
import ml_dtypes
import concourse.bass as bass
import concourse.mybir as mybir
from concourse.bass_utils import run_bass_kernel_spmd

F32 = mybir.dt.float32
BF16 = mybir.dt.bfloat16
I8 = mybir.dt.int8
AF = mybir.ActivationFunctionType
ALU = mybir.AluOpType
AX = mybir.AxisListType

NCORES = 8
D = 2048
T = 8192
TC = T // NCORES
HD = 64
NQH = 4
SC = float(np.sqrt(np.float64(0.125)))
EPS = 1e-5
INTER = 8192
RG = [list(range(NCORES))]

ENGS = ("sync", "gpsimd", "tensor", "scalar", "vector")
SEMS = ("IN", "WL", "SO", "PE", "ACT", "DVE", "CC", "BAR")


class Sch:
    def __init__(self):
        self.prog = {e: [] for e in ENGS}
        self.n = {s: 0 for s in SEMS}
        self.lw = {e: {s: 0 for s in SEMS} for e in ENGS}

    def wait(self, eng, sem, val):
        if not val or val <= self.lw[eng][sem]:
            return
        self.lw[eng][sem] = val
        self.prog[eng].append(("w", sem, val))

    def op(self, eng, fn, inc=None, amt=1):
        if inc:
            self.n[inc] += amt
            self.prog[eng].append(("o", fn, inc, amt))
            return self.n[inc]
        self.prog[eng].append(("o", fn, None, 0))
        return None

    def dma(self, eng, fn, inc="IN"):
        return self.op(eng, fn, inc, 16)

    def multi(self, eng, fn, inc, amt):
        self.n[inc] += amt
        self.prog[eng].append(("m", fn, inc, amt))
        return self.n[inc]

    def barrier(self):
        tgt = self.n["BAR"] + len(ENGS)
        for e in ENGS:
            self.op(e, lambda E: E.nop(), "BAR", 1)
        for e in ENGS:
            self.wait(e, "BAR", tgt)

    def emit(self, block, sems):
        def mkbody(name):
            def run(E):
                for item in self.prog[name]:
                    if item[0] == "w":
                        E.wait_ge(sems[item[1]], item[2])
                    elif item[0] == "m":
                        item[1](E, sems[item[2]])
                    else:
                        _, fn, inc, amt = item
                        instr = fn(E)
                        if inc:
                            instr.then_inc(sems[inc], amt)
            return run

        block.sync(mkbody("sync"))
        block.gpsimd(mkbody("gpsimd"))
        block.tensor(mkbody("tensor"))
        block.scalar(mkbody("scalar"))
        block.vector(mkbody("vector"))


def build_nc(dbg_stop=None):
    nc = bass.Bass("TRN2", target_bir_lowering=False, debug=False)

    xt = nc.dram_tensor("xt", [D, TC], F32, kind="ExternalInput").ap()
    wqkv = nc.dram_tensor("wqkv", [D, 384], F32, kind="ExternalInput").ap()
    wot = nc.dram_tensor("wot", [D, D], F32, kind="ExternalInput").ap()
    wg_t = nc.dram_tensor("wg_t", [INTER, D], I8, kind="ExternalInput").ap()
    wu_t = nc.dram_tensor("wu_t", [INTER, D], I8, kind="ExternalInput").ap()
    wd_t = nc.dram_tensor("wd_t", [512, 64 * 512], I8, kind="ExternalInput").ap()
    sgd = nc.dram_tensor("sgd", [128, 64], F32, kind="ExternalInput").ap()
    sud = nc.dram_tensor("sud", [128, 64], F32, kind="ExternalInput").ap()
    sdd = nc.dram_tensor("sdd", [128, 16], F32, kind="ExternalInput").ap()
    ctokd = nc.dram_tensor("ctokd", [128, 2048], F32, kind="ExternalInput").ap()
    stokd = nc.dram_tensor("stokd", [128, 2048], F32, kind="ExternalInput").ap()
    masksd = nc.dram_tensor("masksd", [128, 8 * 1024], BF16, kind="ExternalInput").ap()
    identd = nc.dram_tensor("identd", [128, 128], BF16, kind="ExternalInput").ap()
    cmiscd = nc.dram_tensor("cmiscd", [65, 256], F32, kind="ExternalInput").ap()
    col32d = nc.dram_tensor("col32d", [128, 1], F32, kind="ExternalInput").ap()
    colbd = nc.dram_tensor("colbd", [128, 1], BF16, kind="ExternalInput").ap()
    rkd = nc.dram_tensor("rkd", [1, 1], mybir.dt.int32, kind="ExternalInput").ap()

    y = nc.dram_tensor("y", [D, TC], F32, kind="ExternalOutput").ap()

    cc1_in = nc.dram_tensor("cc1_in", [D, TC], BF16).ap()
    cc1_out = nc.dram_tensor("cc1_out", [NCORES * D, TC], BF16, addr_space="Shared").ap()
    cc2a_in = nc.dram_tensor("cc2a_in", [2 * HD, T], BF16).ap()
    cc2b_in = nc.dram_tensor("cc2b_in", [2 * HD, T], BF16).ap()
    cc2a_out = nc.dram_tensor("cc2a_out", [NCORES * 2 * HD, T], BF16, addr_space="Shared").ap()
    cc2b_out = nc.dram_tensor("cc2b_out", [NCORES * 2 * HD, T], BF16, addr_space="Shared").ap()
    x2d = nc.dram_tensor("x2d", [D, TC], F32).ap()

    dbg_outs = {}

    def dbg(name, shape, dtype=F32):
        ap = nc.dram_tensor(name, shape, dtype, kind="ExternalOutput").ap()
        dbg_outs[name] = ap
        return ap

    S = Sch()

    with contextlib.ExitStack() as root:
        _uid = [0]

        def sb(stack, name, shape, dtype, side=None):
            _uid[0] += 1
            return stack.enter_context(
                nc.sbuf_tensor(f"{name}_{_uid[0]}", shape, dtype, side=side))[:, :]

        def ps(stack, name, shape, dtype=F32):
            _uid[0] += 1
            return stack.enter_context(
                nc.psum_tensor(f"{name}_{_uid[0]}", shape, dtype))[:, :]

        sems = {s: root.enter_context(nc.semaphore(s)) for s in SEMS}

        s_ident = sb(root, "s_ident", [128, 128], BF16)
        s_cmisc = sb(root, "s_cmisc", [65, 256], F32)
        s_col32 = sb(root, "s_col32", [128, 1], F32)
        s_colb = sb(root, "s_colb", [128, 1], BF16)
        s_sg = sb(root, "s_sg", [128, 64], F32)
        s_su = sb(root, "s_su", [128, 64], F32)
        s_sd = sb(root, "s_sd", [128, 16], F32)
        r_a = sb(root, "r_a", [1, 1024], F32)
        r_b = sb(root, "r_b", [1, 1024], F32)
        r_c = sb(root, "r_c", [1, 1024], F32)
        s_wqkv = sb(root, "s_wqkv", [128, 16 * 384], BF16)
        s_km2 = sb(root, "s_km2", [65, 1], F32)

        for dst, src in ((s_ident, identd), (s_cmisc, cmiscd), (s_col32, col32d),
                         (s_colb, colbd), (s_sg, sgd), (s_su, sud), (s_sd, sdd)):
            S.dma("sync", lambda E, dst=dst, src=src: E.dma_start(out=dst, in_=src))
        S.op("vector", lambda E: E.memset(s_km2[64:65, :], 0.0), "DVE")

        for c in range(16):
            S.dma("gpsimd", lambda E, c=c: E.dma_start(
                out=s_wqkv[:, c * 384:(c + 1) * 384],
                in_=wqkv[c * 128:(c + 1) * 128, :]), "WL")
        wl_wqkv = S.n["WL"]

        def emit_ln_rows(p_sm, p_ssq, W, roff, p_rb, p_bb, s_rstdb, s_bb, pe_stats_done):
            sl = slice(roff, roff + W)
            S.wait("scalar", "PE", pe_stats_done)
            S.op("scalar", lambda E: E.activation(
                out=r_a[:, sl], in_=p_sm, func=AF.Copy, scale=1.0 / D), "ACT")
            a2 = S.op("scalar", lambda E: E.activation(
                out=r_b[:, sl], in_=p_sm, func=AF.Square, scale=1.0 / D), "ACT")
            S.wait("vector", "ACT", a2)
            S.wait("vector", "PE", pe_stats_done)
            d1 = S.op("vector", lambda E: E.tensor_scalar(
                out=r_c[:, sl], in0=p_ssq, scalar1=1.0 / D, scalar2=EPS,
                op0=ALU.mult, op1=ALU.add), "DVE")
            S.wait("vector", "DVE", d1)
            d2 = S.op("vector", lambda E: E.tensor_tensor(
                out=r_c[:, sl], in0=r_c[:, sl], in1=r_b[:, sl], op=ALU.subtract), "DVE")
            S.wait("scalar", "DVE", d2)
            a3 = S.op("scalar", lambda E: E.activation(
                out=r_b[:, sl], in_=r_c[:, sl], func=AF.Sqrt), "ACT")
            S.wait("vector", "ACT", a3)
            d3 = S.op("vector", lambda E: E.reciprocal(
                out=r_c[:, sl], in_=r_b[:, sl]), "DVE")
            S.wait("vector", "DVE", d3)
            d4 = S.op("vector", lambda E: E.tensor_tensor(
                out=r_b[:, sl], in0=r_a[:, sl], in1=r_c[:, sl], op=ALU.mult), "DVE")
            S.wait("tensor", "DVE", d4)
            pe = None
            for o in range(0, W, 512):
                w = min(512, W - o)
                S.op("tensor", lambda E, o=o, w=w: E.matmul(
                    p_rb[:, o:o + w], s_cmisc[0:1, 0:128],
                    r_c[:, roff + o:roff + o + w], start=True, stop=True), "PE")
                pe = S.op("tensor", lambda E, o=o, w=w: E.matmul(
                    p_bb[:, o:o + w], s_cmisc[0:1, 128:256],
                    r_b[:, roff + o:roff + o + w], start=True, stop=True), "PE")
            S.wait("vector", "PE", pe)
            S.op("vector", lambda E: E.tensor_copy(out=s_rstdb[:, 0:W], in_=p_rb), "DVE")
            dve = S.op("vector", lambda E: E.tensor_copy(out=s_bb[:, 0:W], in_=p_bb), "DVE")
            return dict(act=a3, dve=dve, d1=d1)

        with contextlib.ExitStack() as ph1:
            s_x = sb(ph1, "s_x", [128, 16 * 1024], F32)
            s_xln = sb(ph1, "s_xln", [128, 16 * 1024], BF16)
            s_sq1 = sb(ph1, "s_sq1", [128, 2 * 1024], BF16)
            s_t1 = sb(ph1, "s_t1", [128, 1024], F32)
            s_rstdb1 = sb(ph1, "s_rstdb1", [128, 1024], F32)
            s_bb1 = sb(ph1, "s_bb1", [128, 1024], F32)
            p_sm = ps(ph1, "p_sm", [1, 1024])
            p_ssq = ps(ph1, "p_ssq", [1, 1024])
            p_rb = ps(ph1, "p_rb", [128, 1024])
            p_bb = ps(ph1, "p_bb", [128, 1024])

            xload = []
            for c in range(16):
                xload.append(S.dma("sync", lambda E, c=c: E.dma_start(
                    out=s_x[:, c * 1024:(c + 1) * 1024],
                    in_=xt[c * 128:(c + 1) * 128, :])))

            mm_done = [0, 0]
            pe_last = None
            for c in range(16):
                par = c % 2
                xc = s_x[:, c * 1024:(c + 1) * 1024]
                sqc = s_sq1[:, par * 1024:(par + 1) * 1024]
                S.wait("scalar", "IN", xload[c])
                S.wait("scalar", "PE", mm_done[par])
                sq = S.op("scalar", lambda E, xc=xc, sqc=sqc: E.activation(
                    out=sqc, in_=xc, func=AF.Square), "ACT")
                S.wait("tensor", "ACT", sq)
                S.wait("tensor", "IN", xload[c])
                for o in (0, 512):
                    S.op("tensor", lambda E, xc=xc, o=o, c=c: E.matmul(
                        p_sm[:, o:o + 512], s_col32, xc[:, o:o + 512],
                        start=(c == 0), stop=(c == 15)), "PE")
                    pe_last = S.op("tensor", lambda E, sqc=sqc, o=o, c=c: E.matmul(
                        p_ssq[:, o:o + 512], s_colb, sqc[:, o:o + 512],
                        start=(c == 0), stop=(c == 15)), "PE")
                mm_done[par] = pe_last

            g1 = emit_ln_rows(p_sm, p_ssq, 1024, 0, p_rb, p_bb, s_rstdb1, s_bb1, pe_last)

            S.wait("vector", "DVE", g1["dve"])
            dve_norm = None
            for c in range(16):
                xc = s_x[:, c * 1024:(c + 1) * 1024]
                xlc = s_xln[:, c * 1024:(c + 1) * 1024]
                d1 = S.op("vector", lambda E, xc=xc: E.tensor_tensor(
                    out=s_t1, in0=xc, in1=s_rstdb1, op=ALU.mult), "DVE")
                S.wait("vector", "DVE", d1)
                dve_norm = S.op("vector", lambda E, xlc=xlc: E.tensor_tensor(
                    out=xlc, in0=s_t1, in1=s_bb1, op=ALU.add), "DVE")

            S.wait("scalar", "DVE", dve_norm)
            so_xln = S.dma("scalar", lambda E: E.dma_start(
                out=cc1_in.rearrange("(c p) t -> p c t", p=128),
                in_=s_xln.rearrange("p (c t) -> p c t", c=16)), "SO")

            S.wait("gpsimd", "SO", so_xln)
            cc1 = S.op("gpsimd", lambda E: E.collective_compute(
                "AllGather", ALU.bypass, replica_groups=RG,
                ins=[cc1_in.opt()], outs=[cc1_out.opt()]), "CC")

            if dbg_stop == 1:
                d_xln = dbg("d_xln", [D, TC], BF16)
                d_g = dbg("d_g", [NCORES * D, TC], BF16)
                S.wait("gpsimd", "CC", cc1)
                S.dma("gpsimd", lambda E: E.dma_start(
                    out=d_xln.rearrange("(c p) t -> p c t", p=128),
                    in_=s_xln.rearrange("p (c t) -> p c t", c=16)), "SO")
                S.dma("gpsimd", lambda E: E.dma_start(out=d_g, in_=cc1_out), "SO")
                S.wait("gpsimd", "SO", S.n["SO"])
                return _finish(nc, S, sems, root)

        S.barrier()

        qrow = 0
        att = root.enter_context(contextlib.ExitStack())
        s_kt = sb(att, "s_kt", [65, T], BF16)
        s_vt = sb(att, "s_vt", [128, 64 * 65], BF16)
        s_qt = sb(att, "s_qt", [65, NQH * T], BF16)

        S.op("vector", lambda E: E.memset(s_kt[64:65, :], 1.0), "DVE")
        vt_ones = s_vt.rearrange("p (j f) -> p j f", f=65)[:, :, 64:65]
        dve_ph0 = S.op("vector", lambda E: E.memset(vt_ones, 1.0), "DVE")

        with contextlib.ExitStack() as ph2:
            s_ctok = sb(ph2, "s_ctok", [128, 2048], F32)
            s_stok = sb(ph2, "s_stok", [128, 2048], F32)
            s_xg = sb(ph2, "s_xg", [128, 2 * 8192], BF16)
            s_rope = sb(ph2, "s_rope", [128, 2 * 384], BF16)
            s_r1 = sb(ph2, "s_r1", [128, 160], F32)
            s_r2 = sb(ph2, "s_r2", [128, 160], F32)
            s_r3 = sb(ph2, "s_r3", [128, 160], F32)
            s_r4 = sb(ph2, "s_r4", [128, 160], F32)
            s_sqk = sb(ph2, "s_sqk", [64, T], BF16)
            s_kstd = sb(ph2, "s_kstd", [65, 1], F32)
            s_tred = sb(ph2, "s_tred", [65, 2], F32)
            s_brow = sb(ph2, "s_brow", [65, 1024], F32)
            p_qkv0 = ps(ph2, "p_qkv0", [128, 384])
            p_qkv1 = ps(ph2, "p_qkv1", [128, 384])
            p_tr0 = ps(ph2, "p_tr0", [64, 128], BF16)
            p_tr1 = ps(ph2, "p_tr1", [64, 128], BF16)
            p_nrm = ps(ph2, "p_nrm", [65, 1024])
            p_qkv = (p_qkv0, p_qkv1)
            p_tr = (p_tr0, p_tr1)

            S.dma("sync", lambda E: E.dma_start(out=s_ctok, in_=ctokd))
            tabs = S.dma("sync", lambda E: E.dma_start(out=s_stok, in_=stokd))

            xg_load = [0, 0]
            xg_cons = [0, 0]
            qkv_cons = [0, 0]
            tr_evac = [0, 0]
            rope_done_t = {}
            last_tr_pe = {}
            pending_tr = None

            def emit_transposes(tt):
                rp = (tt % 2) * 384
                for j in range(5):
                    part = (tt * 5 + j) % 2
                    S.wait("tensor", "DVE", rope_done_t[tt])
                    S.wait("tensor", "DVE", tr_evac[part])
                    pe_t = S.op("tensor", lambda E, part=part, j=j, rp=rp: E.transpose(
                        p_tr[part], s_rope[:, rp + j * 64: rp + (j + 1) * 64],
                        s_ident), "PE")
                    last_tr_pe[tt] = pe_t
                    S.wait("vector", "PE", pe_t)
                    if j < 4:
                        dst = s_qt[0:64, j * T + tt * 128: j * T + tt * 128 + 128]
                    else:
                        dst = s_kt[0:64, tt * 128: tt * 128 + 128]
                    tr_evac[part] = S.op(
                        "vector", lambda E, part=part, dst=dst: E.tensor_scalar(
                            out=dst, in0=p_tr[part][0:64, :], scalar1=SC, scalar2=None,
                            op0=ALU.mult), "DVE")

            S.wait("sync", "CC", 1)
            for b in range(16):
                par = b % 2
                r, hb = b // 2, b % 2
                S.wait("sync", "PE", xg_cons[par])
                xg_load[par] = S.dma("sync", lambda E, par=par, r=r, hb=hb: E.dma_start(
                    out=s_xg[:, par * 8192:(par + 1) * 8192]
                    .rearrange("p (c t) -> p c t", c=16),
                    in_=cc1_out[r * D:(r + 1) * D, hb * 512:(hb + 1) * 512]
                    .rearrange("(c p) t -> p c t", p=128)))
                for t4 in range(4):
                    tt = b * 4 + t4
                    parp = tt % 2
                    S.wait("tensor", "IN", xg_load[par])
                    S.wait("tensor", "WL", wl_wqkv)
                    S.wait("tensor", "DVE", qkv_cons[parp])
                    pe_p = None
                    for c in range(16):
                        pe_p = S.op("tensor", lambda E, par=par, parp=parp, c=c, t4=t4: E.matmul(
                            p_qkv[parp],
                            s_xg[:, par * 8192 + c * 512 + t4 * 128:
                                 par * 8192 + c * 512 + t4 * 128 + 128],
                            s_wqkv[:, c * 384:(c + 1) * 384],
                            start=(c == 0), stop=(c == 15)), "PE")
                    xg_cons[par] = pe_p
                    if pending_tr is not None:
                        emit_transposes(pending_tr)
                    pq = p_qkv[parp]
                    x1 = pq[:, 0:320].rearrange("p (h x f) -> p h x f", h=5, x=2)[:, :, 0, :]
                    x2v = pq[:, 0:320].rearrange("p (h x f) -> p h x f", h=5, x=2)[:, :, 1, :]
                    ctb = s_ctok[:, tt * 32:(tt + 1) * 32].unsqueeze(1).broadcast_to([128, 5, 32])
                    stb = s_stok[:, tt * 32:(tt + 1) * 32].unsqueeze(1).broadcast_to([128, 5, 32])
                    rp = (tt % 2) * 384
                    rview = s_rope[:, rp:rp + 320].rearrange("p (h x f) -> p h x f", h=5, x=2)
                    o1, o2 = rview[:, :, 0, :], rview[:, :, 1, :]
                    r1v = s_r1.rearrange("p (h f) -> p h f", h=5)
                    r2v = s_r2.rearrange("p (h f) -> p h f", h=5)
                    r3v = s_r3.rearrange("p (h f) -> p h f", h=5)
                    r4v = s_r4.rearrange("p (h f) -> p h f", h=5)
                    S.wait("vector", "IN", tabs)
                    S.wait("vector", "PE", pe_p)
                    S.wait("vector", "PE", last_tr_pe.get(tt - 2, 0))
                    S.op("vector", lambda E, x1=x1, ctb=ctb, r1v=r1v: E.tensor_tensor(
                        out=r1v, in0=x1, in1=ctb, op=ALU.mult), "DVE")
                    S.op("vector", lambda E, x2v=x2v, stb=stb, r2v=r2v: E.tensor_tensor(
                        out=r2v, in0=x2v, in1=stb, op=ALU.mult), "DVE")
                    S.op("vector", lambda E, x1=x1, stb=stb, r3v=r3v: E.tensor_tensor(
                        out=r3v, in0=x1, in1=stb, op=ALU.mult), "DVE")
                    d4 = S.op("vector", lambda E, x2v=x2v, ctb=ctb, r4v=r4v: E.tensor_tensor(
                        out=r4v, in0=x2v, in1=ctb, op=ALU.mult), "DVE")
                    S.wait("vector", "DVE", d4)
                    S.op("vector", lambda E, o1=o1, r1v=r1v, r2v=r2v: E.tensor_tensor(
                        out=o1, in0=r1v, in1=r2v, op=ALU.subtract), "DVE")
                    rope_done_t[tt] = S.op(
                        "vector", lambda E, o2=o2, r3v=r3v, r4v=r4v: E.tensor_tensor(
                            out=o2, in0=r3v, in1=r4v, op=ALU.add), "DVE")
                    qkv_cons[parp] = S.op("vector", lambda E, pq=pq, tt=tt: E.tensor_copy(
                        out=s_vt[:, tt * 65: tt * 65 + 64], in_=pq[:, 320:384]), "DVE")
                    pending_tr = tt
            emit_transposes(pending_tr)

            all_evac = S.n["DVE"]
            S.wait("scalar", "DVE", all_evac)
            a_k = S.op("scalar", lambda E: E.activation(
                out=s_sqk[0:64, :], in_=s_kt[0:64, :], func=AF.Square), "ACT")
            kq_pe = [0, 0]
            S.wait("tensor", "ACT", a_k)
            for s in range(16):
                par = s % 2
                S.wait("tensor", "DVE", S.n["DVE"])
                kq_pe[par] = S.op("tensor", lambda E, s=s, par=par: E.matmul(
                    p_nrm[64:65, par * 512:(par + 1) * 512], s_colb[0:64, :],
                    s_sqk[0:64, s * 512:(s + 1) * 512],
                    start=True, stop=True), "PE")
                S.wait("vector", "PE", kq_pe[par])
                d = S.op("vector", lambda E, par=par: E.tensor_reduce(
                    out=s_tred[64:65, par:par + 1],
                    in_=p_nrm[64:65, par * 512:(par + 1) * 512],
                    axis=AX.X, op=ALU.max), "DVE")
                S.wait("vector", "DVE", d)
                S.op("vector", lambda E, par=par: E.tensor_tensor(
                    out=s_km2[64:65, :], in0=s_km2[64:65, :],
                    in1=s_tred[64:65, par:par + 1], op=ALU.max), "DVE")
            km_done = S.n["DVE"]
            S.wait("scalar", "DVE", km_done)
            a_ks = S.op("scalar", lambda E: E.activation(
                out=s_kstd[64:65, :], in_=s_km2[64:65, :], func=AF.Sqrt), "ACT")

            sq_done = {}
            kq_pe = [S.n["PE"], S.n["PE"]]
            for h in range(NQH):
                S.wait("scalar", "PE", kq_pe[1])
                sq_done[h] = S.op("scalar", lambda E, h=h: E.activation(
                    out=s_sqk[0:64, :], in_=s_qt[0:64, h * T:(h + 1) * T],
                    func=AF.Square), "ACT")
                for s in range(16):
                    par = s % 2
                    off = h * T + s * 512
                    S.wait("tensor", "ACT", sq_done[h])
                    S.wait("tensor", "ACT", S.n["ACT"])
                    kq_pe[par] = S.op("tensor", lambda E, s=s, par=par, h=h: E.matmul(
                        p_nrm[64:65, par * 512:(par + 1) * 512], s_colb[0:64, :],
                        s_sqk[0:64, s * 512:(s + 1) * 512],
                        start=True, stop=True), "PE")
                    S.wait("scalar", "PE", kq_pe[par])
                    S.wait("scalar", "DVE", S.n["DVE"])
                    a_sq = S.op("scalar", lambda E, par=par: E.activation(
                        out=s_brow[64:65, par * 512:(par + 1) * 512],
                        in_=p_nrm[64:65, par * 512:(par + 1) * 512],
                        func=AF.Sqrt), "ACT")
                    S.wait("vector", "ACT", a_sq)
                    qrow = S.op("vector", lambda E, par=par, off=off: E.tensor_scalar(
                        out=s_qt[64:65, off:off + 512],
                        in0=s_brow[64:65, par * 512:(par + 1) * 512],
                        scalar1=s_kstd[64:65, 0:1], scalar2=-1.0,
                        op0=ALU.mult, op1=ALU.mult), "DVE")

            if dbg_stop == 3:
                d_qt = dbg("d_qt", [65, NQH * T], BF16)
                d_kt = dbg("d_kt", [65, T], BF16)
                d_vt = dbg("d_vt", [128, 64 * 65], BF16)
                S.wait("gpsimd", "DVE", qrow)
                S.dma("gpsimd", lambda E: E.dma_start(out=d_qt, in_=s_qt), "SO")
                S.dma("gpsimd", lambda E: E.dma_start(out=d_kt, in_=s_kt), "SO")
                S.dma("gpsimd", lambda E: E.dma_start(out=d_vt, in_=s_vt), "SO")
                S.wait("gpsimd", "SO", S.n["SO"])
                return _finish(nc, S, sems, root)

        s_wo_cm = nc.sbuf_tensor("s_wo", [128, 16 * 1024], BF16, side="right")
        s_wo = s_wo_cm.__enter__()[:, :]
        S.wait("gpsimd", "DVE", qrow)
        wo_done = None
        for ac in range(16):
            wo_done = S.dma("gpsimd", lambda E, ac=ac: E.dma_start(
                out=s_wo[:, ac * 1024:(ac + 1) * 1024],
                in_=wot[ac * 128:(ac + 1) * 128, 0:1024]), "WL")

        S.barrier()

        with contextlib.ExitStack() as ph4:
            s_masks = sb(ph4, "s_masks", [128, 8 * 1024], BF16)
            NSP = 6
            s_p = tuple(sb(ph4, f"s_pp{i}", [128, 1024], BF16) for i in range(NSP))
            s_bc = sb(ph4, "s_bc", [64, 1024], F32)
            s_rec = sb(ph4, "s_rec", [65, 1024], F32)
            s_rec2 = sb(ph4, "s_rec2", [65, 1024], F32)
            s_st0 = sb(ph4, "s_st0", [64, 1024], BF16)
            s_st1 = sb(ph4, "s_st1", [64, 1024], BF16)
            p_sc0 = ps(ph4, "p_sc0", [128, 1024])
            p_sc1 = ps(ph4, "p_sc1", [128, 1024])
            p_pv0 = ps(ph4, "p_pv0", [65, 1024])
            p_pv1 = ps(ph4, "p_pv1", [65, 1024])
            p_sc = (p_sc0, p_sc1)
            p_pv = (p_pv0, p_pv1)
            s_st = (s_st0, s_st1)

            mload = S.dma("sync", lambda E: E.dma_start(out=s_masks, in_=masksd))

            S.wait("tensor", "DVE", qrow)
            S.wait("tensor", "DVE", dve_ph0)

            u = 0
            exp_s = [0, 0]
            pv_by_u = {}
            mask_u = {}
            stage_tt_s = [0, 0]
            stage_so = [0, 0]
            bcast_guard = [0, 0]
            e_idx = 0
            pend_pv = [None, None]
            epi_q = []
            so_a_mark = [0]
            last_pb = [0]

            def emit_pv(hs):
                uu, jj, jmax = pend_pv[hs]
                pend_pv[hs] = None
                if jj == 0:
                    while any(e["hs"] == hs for e in epi_q):
                        pump_epi(1 - hs, exp_s[1 - hs])
                S.wait("tensor", "ACT", exp_s[hs])
                if jj == 0:
                    S.wait("tensor", "DVE", stage_tt_s[hs])
                last = None
                for o in (0, 512):
                    last = S.op("tensor", lambda E, jj=jj, uu=uu, o=o, jmax=jmax, hs=hs: E.matmul(
                        p_pv[hs][:, o:o + 512], s_vt[:, jj * 65:(jj + 1) * 65],
                        s_p[uu % NSP][:, o:o + 512],
                        start=(jj == 0), stop=(jj == jmax - 1)), "PE")
                pv_by_u[uu] = last

            def pump_epi(bank, bank_act_guard):
                if not epi_q:
                    return
                epi = epi_q[0]
                if "rec" not in epi:
                    hs = epi["hs"]
                    S.wait("vector", "PE", epi["pv_last"])
                    S.wait("vector", "PE", last_pb[0])
                    dcl = S.op("vector", lambda E, hs=hs: E.tensor_scalar(
                        out=s_rec2[64:65, :], in0=p_pv[hs][64:65, :],
                        scalar1=1e-35, scalar2=None, op0=ALU.max), "DVE")
                    S.wait("vector", "DVE", dcl)
                    epi["rec"] = S.op("vector", lambda E: E.reciprocal(
                        out=s_rec[64:65, :], in_=s_rec2[64:65, :]), "DVE")
                    return
                epi_q.pop(0)
                hs = epi["hs"]
                par_e = epi["par_e"]
                S.wait("tensor", "DVE", epi["rec"])
                S.wait("tensor", "ACT", bank_act_guard)
                pb = None
                for o in (0, 512):
                    pb = S.op("tensor", lambda E, o=o, bank=bank: E.matmul(
                        p_sc[bank][0:64, o:o + 512], s_cmisc[64:65, 0:64],
                        s_rec[64:65, o:o + 512], start=True, stop=True), "PE")
                last_pb[0] = pb
                S.wait("vector", "PE", pb)
                de = S.op("vector", lambda E, bank=bank: E.tensor_copy(
                    out=s_bc, in_=p_sc[bank][0:64, :]), "DVE")
                bcast_guard[bank] = de
                S.wait("vector", "DVE", de)
                S.wait("vector", "SO", stage_so[par_e])
                st_tt = S.op("vector", lambda E, par_e=par_e, hs=hs: E.tensor_tensor(
                    out=s_st[par_e], in0=p_pv[hs][0:64, :], in1=s_bc, op=ALU.mult), "DVE")
                stage_tt_s[hs] = st_tt
                S.wait("scalar", "DVE", st_tt)
                h, qc = epi["h"], epi["qc"]
                ccin = cc2a_in if h < 2 else cc2b_in
                dst = ccin[(h % 2) * 64:(h % 2) * 64 + 64, qc * 1024:(qc + 1) * 1024]
                stage_so[par_e] = S.dma(
                    "scalar", lambda E, par_e=par_e, dst=dst: E.dma_start(
                        out=dst, in_=s_st[par_e]), "SO")
                if h == 1 and qc == 7:
                    so_a_mark[0] = max(stage_so)

            for hp in range(2):
                for qc in range(8):
                    jmax = 8 * qc + 8
                    for j in range(jmax):
                        for hs in range(2):
                            h = 2 * hp + hs
                            diag = j >= jmax - 8
                            S.wait("tensor", "ACT", exp_s[hs])
                            S.wait("tensor", "DVE", bcast_guard[hs])
                            sc_cnt = None
                            for o in (0, 512):
                                sc_cnt = S.op(
                                    "tensor", lambda E, hs=hs, j=j, h=h, qc=qc, o=o: E.matmul(
                                        p_sc[hs][:, o:o + 512],
                                        s_kt[:, j * 128:(j + 1) * 128],
                                        s_qt[:, h * T + qc * 1024 + o:
                                             h * T + qc * 1024 + o + 512],
                                        start=True, stop=True), "PE")
                            if pend_pv[hs] is not None:
                                emit_pv(hs)
                            if diag:
                                mo = (j - (jmax - 8)) * 1024
                                S.wait("vector", "IN", mload)
                                S.wait("vector", "PE", sc_cnt)
                                mask_u[u] = S.op(
                                    "vector", lambda E, hs=hs, mo=mo: E.tensor_tensor(
                                        out=p_sc[hs], in0=p_sc[hs],
                                        in1=s_masks[:, mo:mo + 1024], op=ALU.add), "DVE")
                            S.wait("scalar", "PE", sc_cnt)
                            if diag:
                                S.wait("scalar", "DVE", mask_u[u])
                            S.wait("scalar", "PE", pv_by_u.get(u - (NSP - 1), 0))
                            exp_s[hs] = S.op("scalar", lambda E, hs=hs, u=u: E.activation(
                                out=s_p[u % NSP], in_=p_sc[hs], func=AF.Exp), "ACT")
                            pend_pv[hs] = (u, j, jmax)
                            pump_epi(hs, exp_s[hs])
                            u += 1
                    for hs in range(2):
                        emit_pv(hs)
                        epi_q.append(dict(hs=hs, par_e=e_idx % 2,
                                          pv_last=pv_by_u[u - 2 + hs],
                                          h=2 * hp + hs, qc=qc))
                        e_idx += 1

            while epi_q:
                pump_epi(0, S.n["ACT"])
                pump_epi(1, S.n["ACT"])

            S.wait("gpsimd", "SO", so_a_mark[0])
            cc2a = S.op("gpsimd", lambda E: E.collective_compute(
                "AllGather", ALU.bypass, replica_groups=RG,
                ins=[cc2a_in.opt()], outs=[cc2a_out.opt()]), "CC")
            S.wait("gpsimd", "SO", S.n["SO"])
            cc2b = S.op("gpsimd", lambda E: E.collective_compute(
                "AllGather", ALU.bypass, replica_groups=RG,
                ins=[cc2b_in.opt()], outs=[cc2b_out.opt()]), "CC")

            if dbg_stop == 4:
                d_at = dbg("d_at", [2 * NCORES * 2 * HD, T], F32)
                S.wait("gpsimd", "CC", cc2b)
                S.dma("gpsimd", lambda E: E.dma_start(
                    out=d_at[0:1024, :], in_=cc2a_out), "SO")
                S.dma("gpsimd", lambda E: E.dma_start(
                    out=d_at[1024:2048, :], in_=cc2b_out), "SO")
                S.wait("gpsimd", "SO", S.n["SO"])
                return _finish(nc, S, sems, root)

        att.close()
        S.barrier()

        with contextlib.ExitStack() as ph5:
            s_x2 = sb(ph5, "s_x2", [128, 16 * 1024], F32)
            s_attno = sb(ph5, "s_attno", [128, 16 * 1024], BF16)
            s_sq5 = sb(ph5, "s_sq5", [128, 2 * 512], BF16)
            s_xt5 = sb(ph5, "s_xt5", [128, 2 * 1024], F32)
            s_t5 = sb(ph5, "s_t5", [128, 512], F32)
            s_rstdb5 = sb(ph5, "s_rstdb5", [128, 2 * 512], F32)
            s_bb5 = sb(ph5, "s_bb5", [128, 2 * 512], F32)
            p_wo0 = ps(ph5, "p_wo0", [128, 512])
            p_wo1 = ps(ph5, "p_wo1", [128, 512])
            p_sm5 = ps(ph5, "p_sm5", [1, 1024])
            p_ssq5 = ps(ph5, "p_ssq5", [1, 1024])
            p_rb5 = ps(ph5, "p_rb5", [128, 512])
            p_bb5p = ps(ph5, "p_bb5p", [128, 512])
            p_wo = (p_wo0, p_wo1)

            S.wait("gpsimd", "CC", cc2b)

            def attno_loads(E, sem):
                with contextlib.ExitStack() as regs:
                    rk = regs.enter_context(E.register("rk"))
                    rc = regs.enter_context(E.register("rc"))
                    E.load(rk, rkd[0:1, 0:1])
                    E.reg_mul(rc, rk, TC)
                    coff = E.snap(rc, min_val=0, max_val=(NCORES - 1) * TC)
                    for ac in range(8):
                        E.dma_start(
                            out=s_attno[:, ac * 1024:(ac + 1) * 1024],
                            in_=cc2a_out[ac * 128:(ac + 1) * 128,
                                         bass.ds(coff, TC)]).then_inc(sem, 16)
                    for ac in range(8):
                        E.dma_start(
                            out=s_attno[:, (8 + ac) * 1024:(9 + ac) * 1024],
                            in_=cc2b_out[ac * 128:(ac + 1) * 128,
                                         bass.ds(coff, TC)]).then_inc(sem, 16)
            at_load = S.multi("gpsimd", attno_loads, "WL", 16 * 16)

            evac5 = [0, 0]
            sum_pe = None
            half0_pe = None
            pidx = 0
            for oh in range(2):
                if oh == 1:
                    S.wait("gpsimd", "PE", half0_pe)
                    for ac in range(16):
                        wo_done = S.dma("gpsimd", lambda E, ac=ac: E.dma_start(
                            out=s_wo[:, ac * 1024:(ac + 1) * 1024],
                            in_=wot[ac * 128:(ac + 1) * 128, 1024:2048]), "WL")
                for qh in range(2):
                    for ot in range(8):
                        og = oh * 8 + ot
                        par = pidx % 2
                        pidx += 1
                        S.wait("tensor", "WL", wo_done)
                        S.wait("tensor", "WL", at_load)
                        S.wait("tensor", "ACT", evac5[par])
                        mm = None
                        for ac in range(16):
                            mm = S.op("tensor",
                                      lambda E, par=par, ac=ac, ot=ot, qh=qh: E.matmul(
                                          p_wo[par],
                                          s_wo[:, ac * 1024 + ot * 128:
                                               ac * 1024 + ot * 128 + 128],
                                          s_attno[:, ac * 1024 + qh * 512:
                                                  ac * 1024 + qh * 512 + 512],
                                          start=(ac == 0), stop=(ac == 15)), "PE")
                        dst = s_x2[:, og * 1024 + qh * 512: og * 1024 + qh * 512 + 512]
                        sqv = s_sq5[:, par * 512:(par + 1) * 512]
                        S.wait("scalar", "PE", mm)
                        S.op("scalar", lambda E, par=par, dst=dst: E.activation(
                            out=dst, in_=p_wo[par], func=AF.Copy), "ACT")
                        evac5[par] = S.op("scalar", lambda E, par=par, sqv=sqv: E.activation(
                            out=sqv, in_=p_wo[par], func=AF.Square), "ACT")
                        S.wait("tensor", "ACT", evac5[par])
                        S.op("tensor", lambda E, qh=qh, dst=dst, og=og: E.matmul(
                            p_sm5[:, qh * 512:(qh + 1) * 512], s_col32, dst,
                            start=(og == 0), stop=(og == 15)), "PE")
                        sum_pe = S.op("tensor", lambda E, qh=qh, sqv=sqv, og=og: E.matmul(
                            p_ssq5[:, qh * 512:(qh + 1) * 512], s_colb, sqv,
                            start=(og == 0), stop=(og == 15)), "PE")
                if oh == 0:
                    half0_pe = S.n["PE"]

            g5 = None
            for qh in range(2):
                g5 = emit_ln_rows(
                    p_sm5[:, qh * 512:(qh + 1) * 512],
                    p_ssq5[:, qh * 512:(qh + 1) * 512],
                    512, qh * 512, p_rb5, p_bb5p,
                    s_rstdb5[:, qh * 512:(qh + 1) * 512],
                    s_bb5[:, qh * 512:(qh + 1) * 512], sum_pe)

            S.wait("vector", "DVE", g5["dve"])
            xt5_cons = [0, 0]
            xload5 = [0, 0]
            x2_dve = None
            for c in range(16):
                par = c % 2
                S.wait("sync", "DVE", xt5_cons[par])
                xload5[par] = S.dma("sync", lambda E, c=c, par=par: E.dma_start(
                    out=s_xt5[:, par * 1024:(par + 1) * 1024],
                    in_=xt[c * 128:(c + 1) * 128, :]))
                for qh in range(2):
                    xc = s_x2[:, c * 1024 + qh * 512: c * 1024 + qh * 512 + 512]
                    d1 = S.op("vector", lambda E, xc=xc, qh=qh: E.tensor_tensor(
                        out=s_t5, in0=xc, in1=s_rstdb5[:, qh * 512:(qh + 1) * 512],
                        op=ALU.mult), "DVE")
                    S.wait("vector", "DVE", d1)
                    d2 = S.op("vector", lambda E, qh=qh: E.tensor_tensor(
                        out=s_t5, in0=s_t5, in1=s_bb5[:, qh * 512:(qh + 1) * 512],
                        op=ALU.add), "DVE")
                    S.wait("vector", "DVE", d2)
                    S.wait("vector", "IN", xload5[par])
                    x2_dve = S.op("vector", lambda E, xc=xc, par=par, qh=qh: E.tensor_tensor(
                        out=xc, in0=s_t5,
                        in1=s_xt5[:, par * 1024 + qh * 512: par * 1024 + qh * 512 + 512],
                        op=ALU.add), "DVE")
                    xt5_cons[par] = x2_dve

            S.wait("scalar", "DVE", x2_dve)
            so_x2 = S.dma("scalar", lambda E: E.dma_start(
                out=x2d.rearrange("(c p) t -> p c t", p=128),
                in_=s_x2.rearrange("p (c t) -> p c t", c=16)), "SO")

            if dbg_stop == 5:
                d_x2 = dbg("d_x2", [D, TC], F32)
                S.wait("gpsimd", "SO", so_x2)
                S.dma("gpsimd", lambda E: E.dma_start(out=d_x2, in_=x2d), "SO")
                S.wait("gpsimd", "SO", S.n["SO"])
                return _finish(nc, S, sems, root)

        s_wo_cm.__exit__(None, None, None)
        S.barrier()

        y_so = [0, 0]
        for qh in range(2):
            S.barrier()
            with contextlib.ExitStack() as ph6:
                s_fuse = sb(ph6, "s_fuse", [128, 64 * 512], BF16)
                s_sq6 = sb(ph6, "s_sq6", [128, 2 * 512], BF16)
                s_x2s = sb(ph6, "s_x2s", [128, 2 * 512], F32)
                s_t6 = sb(ph6, "s_t6", [128, 512], F32)
                s_rstdb6 = sb(ph6, "s_rstdb6", [128, 512], F32)
                s_bb6 = sb(ph6, "s_bb6", [128, 512], F32)
                s_y0 = sb(ph6, "s_y0", [128, 512], F32)
                s_y1 = sb(ph6, "s_y1", [128, 512], F32)
                p_sm6 = ps(ph6, "p_sm6", [1, 512])
                p_ssq6 = ps(ph6, "p_ssq6", [1, 512])
                p_rb6 = ps(ph6, "p_rb6", [128, 512])
                p_bb6p = ps(ph6, "p_bb6p", [128, 512])
                s_y = (s_y0, s_y1)

                def x2_chunk(c):
                    return x2d[c * 128:(c + 1) * 128, qh * 512:(qh + 1) * 512]

                S.wait("sync", "SO", so_x2)
                sq_pe = [0, 0]
                x2_cons = [0, 0]
                mm_last = None
                for c in range(16):
                    par = c % 2
                    xv = s_x2s[:, par * 512:(par + 1) * 512]
                    sqv = s_sq6[:, par * 512:(par + 1) * 512]
                    src_c = x2_chunk(c)
                    S.wait("sync", "PE", x2_cons[par])
                    ld = S.dma("sync", lambda E, src_c=src_c, xv=xv: E.dma_start(
                        out=xv, in_=src_c))
                    S.wait("scalar", "IN", ld)
                    S.wait("scalar", "PE", sq_pe[par])
                    a = S.op("scalar", lambda E, xv=xv, sqv=sqv: E.activation(
                        out=sqv, in_=xv, func=AF.Square), "ACT")
                    S.wait("tensor", "ACT", a)
                    S.wait("tensor", "IN", ld)
                    S.op("tensor", lambda E, xv=xv, c=c: E.matmul(
                        p_sm6, s_col32, xv, start=(c == 0), stop=(c == 15)), "PE")
                    mm_last = S.op("tensor", lambda E, sqv=sqv, c=c: E.matmul(
                        p_ssq6, s_colb, sqv, start=(c == 0), stop=(c == 15)), "PE")
                    sq_pe[par] = mm_last
                    x2_cons[par] = mm_last
                g3 = emit_ln_rows(p_sm6, p_ssq6, 512, 0, p_rb6, p_bb6p,
                                  s_rstdb6, s_bb6, mm_last)

                with contextlib.ExitStack() as ph6a:
                    s_xln3 = sb(ph6a, "s_xln3", [128, 16 * 512], BF16)
                    s_wgu0 = sb(ph6a, "s_wgu0", [128, 2 * 2048], BF16)
                    s_wgu1 = sb(ph6a, "s_wgu1", [128, 2 * 2048], BF16)
                    s_ga = sb(ph6a, "s_ga", [128, 2 * 512], F32)
                    s_ua = sb(ph6a, "s_ua", [128, 2 * 512], F32)
                    p_g0 = ps(ph6a, "p_g0", [128, 512])
                    p_g1 = ps(ph6a, "p_g1", [128, 512])
                    p_u0 = ps(ph6a, "p_u0", [128, 512])
                    p_u1 = ps(ph6a, "p_u1", [128, 512])
                    s_wgu = (s_wgu0, s_wgu1)
                    p_gu = ((p_g0, p_u0), (p_g1, p_u1))

                    S.wait("vector", "DVE", g3["dve"])
                    S.wait("sync", "PE", mm_last)
                    xln3_done = None
                    x2n_cons = [0, 0]
                    for c in range(16):
                        par = c % 2
                        xv = s_x2s[:, par * 512:(par + 1) * 512]
                        src_c = x2_chunk(c)
                        S.wait("sync", "DVE", x2n_cons[par])
                        ld = S.dma("sync", lambda E, src_c=src_c, xv=xv: E.dma_start(
                            out=xv, in_=src_c))
                        S.wait("vector", "IN", ld)
                        d1 = S.op("vector", lambda E, xv=xv: E.tensor_tensor(
                            out=s_t6, in0=xv, in1=s_rstdb6, op=ALU.mult), "DVE")
                        S.wait("vector", "DVE", d1)
                        xln3_done = S.op("vector", lambda E, c=c: E.tensor_tensor(
                            out=s_xln3[:, c * 512:(c + 1) * 512], in0=s_t6, in1=s_bb6,
                            op=ALU.add), "DVE")
                        x2n_cons[par] = xln3_done

                    gu_pe = [0, 0]
                    fuse_dve = [0, 0]
                    for it in range(64):
                        par = it % 2
                        S.wait("gpsimd", "PE", gu_pe[par])
                        S.dma("gpsimd", lambda E, it=it, par=par: E.dma_start(
                            out=s_wgu[par][:, 0:2048],
                            in_=wg_t[it * 128:(it + 1) * 128, :]), "WL")
                        wl = S.dma("gpsimd", lambda E, it=it, par=par: E.dma_start(
                            out=s_wgu[par][:, 2048:4096],
                            in_=wu_t[it * 128:(it + 1) * 128, :]), "WL")
                        S.wait("tensor", "WL", wl)
                        S.wait("tensor", "DVE", xln3_done)
                        S.wait("tensor", "DVE", fuse_dve[par])
                        mmg = None
                        for c in range(16):
                            S.op("tensor", lambda E, par=par, c=c: E.matmul(
                                p_gu[par][0], s_wgu[par][:, c * 128:(c + 1) * 128],
                                s_xln3[:, c * 512:(c + 1) * 512],
                                start=(c == 0), stop=(c == 15)), "PE")
                            mmg = S.op("tensor", lambda E, par=par, c=c: E.matmul(
                                p_gu[par][1],
                                s_wgu[par][:, 2048 + c * 128: 2048 + c * 128 + 128],
                                s_xln3[:, c * 512:(c + 1) * 512],
                                start=(c == 0), stop=(c == 15)), "PE")
                        gu_pe[par] = mmg
                        gav = s_ga[:, par * 512:(par + 1) * 512]
                        uav = s_ua[:, par * 512:(par + 1) * 512]
                        S.wait("scalar", "PE", mmg)
                        S.wait("scalar", "DVE", fuse_dve[par])
                        S.op("scalar", lambda E, par=par, it=it, gav=gav: E.activation(
                            out=gav, in_=p_gu[par][0], func=AF.Silu,
                            scale=s_sg[:, it:it + 1]), "ACT")
                        ga = S.op("scalar", lambda E, par=par, it=it, uav=uav: E.activation(
                            out=uav, in_=p_gu[par][1], func=AF.Copy,
                            scale=s_su[:, it:it + 1]), "ACT")
                        S.wait("vector", "ACT", ga)
                        fuse_dve[par] = S.op(
                            "vector", lambda E, it=it, gav=gav, uav=uav: E.tensor_tensor(
                                out=s_fuse[:, it * 512:(it + 1) * 512], in0=gav, in1=uav,
                                op=ALU.mult), "DVE")
                    fuse_all = S.n["DVE"]
                    gu_pe_all = S.n["PE"]
                    gu_act_all = S.n["ACT"]
                    if dbg_stop == 60 and qh == 0:
                        d_xln3 = dbg("d_xln3", [128, 16 * 512], BF16)
                        d_fuse = dbg("d_fuse", [128, 64 * 512], BF16)
                        d_x2q = dbg("d_x2q", [D, TC], F32)
                        S.wait("gpsimd", "DVE", fuse_all)
                        S.dma("gpsimd", lambda E: E.dma_start(out=d_x2q, in_=x2d), "SO")
                        S.dma("gpsimd", lambda E: E.dma_start(out=d_xln3, in_=s_xln3), "SO")
                        S.dma("gpsimd", lambda E: E.dma_start(out=d_fuse, in_=s_fuse), "SO")
                        S.wait("gpsimd", "SO", S.n["SO"])
                        return _finish(nc, S, sems, root)

                with contextlib.ExitStack() as ph6b:
                    s_wdh0 = sb(ph6b, "s_wdh0", [128, 32 * 512], BF16)
                    s_wdh1 = sb(ph6b, "s_wdh1", [128, 32 * 512], BF16)
                    s_ffno = sb(ph6b, "s_ffno", [128, 16 * 512], F32)
                    p_d0 = ps(ph6b, "p_d0", [128, 512])
                    p_d1 = ps(ph6b, "p_d1", [128, 512])
                    p_d2 = ps(ph6b, "p_d2", [128, 512])
                    p_d3 = ps(ph6b, "p_d3", [128, 512])
                    s_wdh = (s_wdh0, s_wdh1)
                    p_d = (p_d0, p_d1, p_d2, p_d3)

                    S.wait("gpsimd", "DVE", fuse_all)
                    wd_pe = [gu_pe_all, gu_pe_all]
                    evac6_guard = gu_act_all
                    sq6_pe = [0, 0]
                    sum_pe6 = None
                    ln3_guarded = False
                    for g4 in range(4):
                        wd_wl = [0, 0]
                        for half in range(2):
                            S.wait("gpsimd", "PE", wd_pe[half])
                            wd_wl[half] = S.dma(
                                "gpsimd", lambda E, g4=g4, half=half: E.dma_start(
                                    out=s_wdh[half],
                                    in_=wd_t[g4 * 128:(g4 + 1) * 128,
                                             half * 16384:(half + 1) * 16384],
                                    max_dma_last_dim=2048), "WL")
                        S.wait("tensor", "DVE", fuse_all)
                        S.wait("tensor", "ACT", evac6_guard)
                        dn_mm = None
                        for ic in range(64):
                            half = ic // 32
                            S.wait("tensor", "WL", wd_wl[half])
                            for ot2 in range(4):
                                dn_mm = S.op(
                                    "tensor", lambda E, half=half, ic=ic, ot2=ot2: E.matmul(
                                        p_d[ot2],
                                        s_wdh[half][:, (ic % 32) * 512 + ot2 * 128:
                                                    (ic % 32) * 512 + ot2 * 128 + 128],
                                        s_fuse[:, ic * 512:(ic + 1) * 512],
                                        start=(ic == 0), stop=(ic == 63)), "PE")
                            if ic == 31:
                                wd_pe[0] = dn_mm
                        wd_pe[1] = dn_mm
                        for ot2 in range(4):
                            og = g4 * 4 + ot2
                            par = og % 2
                            dst = s_ffno[:, og * 512:(og + 1) * 512]
                            sqv = s_sq6[:, par * 512:(par + 1) * 512]
                            S.wait("scalar", "PE", dn_mm)
                            S.wait("scalar", "PE", sq6_pe[par])
                            S.op("scalar", lambda E, ot2=ot2, og=og, dst=dst: E.activation(
                                out=dst, in_=p_d[ot2], func=AF.Copy,
                                scale=s_sd[:, og:og + 1]), "ACT")
                            ev = S.op("scalar", lambda E, ot2=ot2, og=og, sqv=sqv: E.activation(
                                out=sqv, in_=p_d[ot2], func=AF.Square,
                                scale=s_sd[:, og:og + 1]), "ACT")
                            evac6_guard = ev
                            S.wait("tensor", "ACT", ev)
                            if not ln3_guarded:
                                S.wait("tensor", "DVE", g3["d1"])
                                ln3_guarded = True
                            S.op("tensor", lambda E, og=og, dst=dst: E.matmul(
                                p_sm6, s_col32, dst, start=(og == 0), stop=(og == 15)), "PE")
                            sum_pe6 = S.op("tensor", lambda E, og=og, sqv=sqv: E.matmul(
                                p_ssq6, s_colb, sqv, start=(og == 0), stop=(og == 15)), "PE")
                            sq6_pe[par] = sum_pe6

                    if dbg_stop == 61 and qh == 0:
                        d_ffno = dbg("d_ffno", [128, 16 * 512], F32)
                        S.wait("gpsimd", "ACT", evac6_guard)
                        S.wait("gpsimd", "PE", sum_pe6)
                        S.dma("gpsimd", lambda E: E.dma_start(out=d_ffno, in_=s_ffno), "SO")
                        S.wait("gpsimd", "SO", S.n["SO"])
                        return _finish(nc, S, sems, root)
                    g4r = emit_ln_rows(p_sm6, p_ssq6, 512, 512, p_rb6, p_bb6p,
                                       s_rstdb6, s_bb6, sum_pe6)
                    S.wait("vector", "DVE", g4r["dve"])
                    S.wait("sync", "DVE", g4r["dve"])
                    x2f_cons = [0, 0]
                    for og in range(16):
                        par = og % 2
                        fv = s_ffno[:, og * 512:(og + 1) * 512]
                        xv = s_x2s[:, par * 512:(par + 1) * 512]
                        src_c = x2_chunk(og)
                        S.wait("sync", "DVE", x2f_cons[par])
                        ld = S.dma("sync", lambda E, src_c=src_c, xv=xv: E.dma_start(
                            out=xv, in_=src_c))
                        d1 = S.op("vector", lambda E, fv=fv: E.tensor_tensor(
                            out=s_t6, in0=fv, in1=s_rstdb6, op=ALU.mult), "DVE")
                        S.wait("vector", "DVE", d1)
                        d2 = S.op("vector", lambda E: E.tensor_tensor(
                            out=s_t6, in0=s_t6, in1=s_bb6, op=ALU.add), "DVE")
                        S.wait("vector", "DVE", d2)
                        S.wait("vector", "IN", ld)
                        S.wait("vector", "SO", y_so[par])
                        d3 = S.op("vector", lambda E, xv=xv, par=par: E.tensor_tensor(
                            out=s_y[par], in0=s_t6, in1=xv, op=ALU.add), "DVE")
                        x2f_cons[par] = d3
                        S.wait("scalar", "DVE", d3)
                        y_so[par] = S.dma(
                            "scalar", lambda E, par=par, og=og, qh=qh: E.dma_start(
                                out=y[og * 128:(og + 1) * 128, qh * 512:(qh + 1) * 512],
                                in_=s_y[par]), "SO")

        S.wait("scalar", "SO", S.n["SO"])

        return _finish(nc, S, sems, root)


def _finish(nc, S, sems, root):
    block = root.enter_context(nc.Block())
    S.emit(block, sems)
    return nc



def _perm_half(block):
    return np.concatenate([block[:, 0::2], block[:, 1::2]], axis=1)


def host_prep(L0_attn_x, L0_attn_Wq, L0_attn_Wk, L0_attn_Wv, L0_attn_Wo,
              L0_ffn_w_gate, L0_ffn_w_up, L0_ffn_w_down,
              L0_ffn_s_gate, L0_ffn_s_up, L0_ffn_s_down):
    f32 = np.float32
    xT = np.ascontiguousarray(np.asarray(L0_attn_x, dtype=f32)[0].T)
    WqT = np.asarray(L0_attn_Wq, dtype=f32).T
    WkT = np.asarray(L0_attn_Wk, dtype=f32).T
    WvT = np.asarray(L0_attn_Wv, dtype=f32).T
    woT_nat = np.asarray(L0_attn_Wo, dtype=f32).T
    order = [rank * 256 + (2 * half + h2) * 64 + dd
             for half in range(2) for rank in range(NCORES)
             for h2 in range(2) for dd in range(64)]
    woT = np.ascontiguousarray(woT_nat[order, :])

    wg = np.asarray(L0_ffn_w_gate)
    wu = np.asarray(L0_ffn_w_up)
    wd = np.asarray(L0_ffn_w_down)
    wg_t = np.ascontiguousarray(
        wg.reshape(16, 128, 64, 128).transpose(2, 1, 0, 3).reshape(INTER, D))
    wu_t = np.ascontiguousarray(
        wu.reshape(16, 128, 64, 128).transpose(2, 1, 0, 3).reshape(INTER, D))
    wd_t = np.ascontiguousarray(
        wd.reshape(64, 128, 4, 512).transpose(2, 1, 0, 3).reshape(512, 64 * 512))

    sg = np.ascontiguousarray(np.asarray(L0_ffn_s_gate, dtype=f32).reshape(64, 128).T)
    su = np.ascontiguousarray(np.asarray(L0_ffn_s_up, dtype=f32).reshape(64, 128).T)
    sd = np.ascontiguousarray(np.asarray(L0_ffn_s_down, dtype=f32).reshape(16, 128).T)

    inv = 1.0 / 10000.0 ** (np.arange(0, HD, 2, dtype=np.float64) / HD)
    pos = np.arange(T, dtype=np.float64)
    f = pos[:, None] * inv[None, :]
    ctok = np.ascontiguousarray(
        np.cos(f).astype(f32).reshape(64, 128, 32).transpose(1, 0, 2).reshape(128, 2048))
    stok = np.ascontiguousarray(
        np.sin(f).astype(f32).reshape(64, 128, 32).transpose(1, 0, 2).reshape(128, 2048))

    l_i = np.arange(128)[:, None]
    q_i = np.arange(1024)[None, :]
    masks = np.concatenate(
        [np.where(m * 128 + l_i > q_i, -1e9, 0.0) for m in range(8)],
        axis=1).astype(ml_dtypes.bfloat16)

    ident = np.eye(128, dtype=ml_dtypes.bfloat16)
    cmisc = np.concatenate([np.ones((65, 128), f32), -np.ones((65, 128), f32)], axis=1)
    col32 = np.ones((128, 1), f32)
    colb = np.ones((128, 1), ml_dtypes.bfloat16)

    in_maps = []
    for c in range(NCORES):
        heads = []
        for j in range(NQH):
            h = 4 * c + j
            heads.append(_perm_half(WqT[:, 64 * h:64 * h + 64]))
        heads.append(_perm_half(WkT[:, 64 * c:64 * c + 64]))
        heads.append(WvT[:, 64 * c:64 * c + 64])
        wqkv_c = np.ascontiguousarray(np.concatenate(heads, axis=1))
        in_maps.append(dict(
            xt=np.ascontiguousarray(xT[:, TC * c:TC * (c + 1)]),
            wqkv=wqkv_c, wot=woT, wg_t=wg_t, wu_t=wu_t, wd_t=wd_t,
            sgd=sg, sud=su, sdd=sd, ctokd=ctok, stokd=stok, masksd=masks,
            identd=ident, cmiscd=cmisc, col32d=col32, colbd=colb,
            rkd=np.array([[c]], dtype=np.int32),
        ))
    return in_maps


def kernel(**inputs):
    in_maps = host_prep(**inputs)
    nc = build_nc()
    res = run_bass_kernel_spmd(nc, in_maps, core_ids=list(range(NCORES)))
    out = np.empty((1, T, D), dtype=np.float32)
    for c in range(NCORES):
        out[0, TC * c:TC * (c + 1), :] = res.results[c]["y"].T
    return out
